# version 31
# baseline (speedup 1.0000x reference)
import numpy as np
import ml_dtypes
from contextlib import ExitStack

import concourse.bass as bass
import concourse.mybir as mybir
from concourse.bass_utils import run_bass_kernel_spmd

F32 = mybir.dt.float32
BF16 = mybir.dt.bfloat16
I32 = mybir.dt.int32
AF = mybir.ActivationFunctionType
ALU = mybir.AluOpType

PE, ACT, DVE, POOL, SYNC = "pe", "act", "dve", "pool", "sync"
ENG_SEM = {PE: "pe", ACT: "act", DVE: "dve", POOL: "pool"}

REAL_CFG = dict(D=4096, S=4096, KVH=8, G=4, B=4, EPS=1e-5)
DEBUG_SCRATCH = False
DEBUG_NO_LORA = False
DEBUG_P3_SERIAL = False


def cfg_derived(cfg):
    c = dict(cfg)
    c["FC"] = c["D"] // 128
    c["NBLK"] = c["S"] // 128
    c["SLOTS"] = c["NBLK"] // 2
    c["OWN"] = c["S"] // 2
    c["DKV"] = c["KVH"] * 128
    c["DQ"] = c["KVH"] * c["G"] * 128
    c["HC"] = c["DQ"] // 128
    assert c["DQ"] == c["D"], "design assumes DQ == D"
    c["NT1"] = c["S"] // 512
    c["NT2"] = c["OWN"] // 512
    c["TCH"] = c["OWN"] // 128
    c["NOT"] = c["D"] // 512
    return c


def own_blocks(parity, nblk):
    keep = (0, 3) if parity == 0 else (1, 2)
    return [j for j in range(nblk) if j % 4 in keep]


class Prog:

    def __init__(self, nc, block, sems):
        self.nc = nc
        self.block = block
        self.sems = sems
        self.L = {k: [] for k in (PE, ACT, DVE, POOL, SYNC)}
        self.n = {"pe": 0, "act": 0, "dve": 0, "pool": 0,
                  "qx0": 0, "qx1": 0, "qkv0": 0, "qkv1": 0,
                  "qw0": 0, "qw1": 0, "st0": 0, "st1": 0,
                  "st2": 0, "st3": 0, "qld": 0, "cc": 0}
        self.lastw = {k: {} for k in (PE, ACT, DVE, POOL, SYNC)}

    def add(self, eng, fn, waits=(), inc="auto", drain=False, drain_inc=False):
        if inc == "auto":
            inc = (ENG_SEM[eng], 1) if eng in ENG_SEM else None
        if inc is not None and eng in (ACT, DVE, POOL) and inc[0] == ENG_SEM[eng]:
            drain_inc = True
        tok = None
        if inc is not None:
            self.n[inc[0]] += inc[1]
            tok = self.n[inc[0]]
        self.L[eng].append((fn, tuple(waits), inc, drain, drain_inc))
        return tok

    def dma(self, eng, q, fn, waits=()):
        w = list(waits) + [(q, self.n[q])]
        return self.add(eng, fn, waits=w, inc=(q, 16))

    def barrier_waits(self):
        return [(k, v) for k, v in self.n.items() if v > 0 and k != "cc"]

    def flush(self):
        sems = self.sems

        def runner(lst, engkey):
            last = self.lastw[engkey]

            def _(e):
                for fn, waits, inc, drain, drain_inc in lst:
                    for sk, v in waits:
                        if v <= 0:
                            continue
                        if engkey in ENG_SEM and sk == ENG_SEM[engkey]:
                            continue
                        if last.get(sk, -1) >= v:
                            continue
                        e.wait_ge(sems[sk], v)
                        last[sk] = v
                    if drain:
                        e.drain()
                    inst = fn(e)
                    if inc is not None:
                        if drain_inc:
                            e.drain().then_inc(sems[inc[0]], inc[1])
                        else:
                            inst.then_inc(sems[inc[0]], inc[1])
            return _

        if self.L[PE]:
            self.block.tensor(runner(self.L[PE], PE))
        if self.L[ACT]:
            self.block.scalar(runner(self.L[ACT], ACT))
        if self.L[DVE]:
            self.block.vector(runner(self.L[DVE], DVE))
        if self.L[POOL]:
            self.block.gpsimd(runner(self.L[POOL], POOL))
        if self.L[SYNC]:
            self.block.sync(runner(self.L[SYNC], SYNC))
        self.L = {k: [] for k in (PE, ACT, DVE, POOL, SYNC)}


def build_nc(cfg):
    c = cfg_derived(cfg)
    D, S, KVH, G = c["D"], c["S"], c["KVH"], c["G"]
    FC, NBLK, SLOTS, OWN = c["FC"], c["NBLK"], c["SLOTS"], c["OWN"]
    DKV, HC = c["DKV"], c["HC"]
    NT1, NT2, TCH, NOT = c["NT1"], c["NT2"], c["TCH"], c["NOT"]
    EPS = c["EPS"]
    ISCALE = float(1.0 / np.sqrt(128.0))
    invD = 1.0 / D
    invD2 = invD * invD

    nc = bass.Bass("TRN2", target_bir_lowering=False, debug=False)

    x_half = nc.dram_tensor("x_half", [S // 2, D], F32, kind="ExternalInput")
    x_own = nc.dram_tensor("x_own", [OWN, D], F32, kind="ExternalInput")
    wk_t = nc.dram_tensor("wk_t", [128, FC, DKV], BF16, kind="ExternalInput")
    wv_t = nc.dram_tensor("wv_t", [128, FC, DKV], BF16, kind="ExternalInput")
    wq_t = nc.dram_tensor("wq_t", [HC, 128, FC, 128], BF16, kind="ExternalInput")
    wf_t = nc.dram_tensor("wf_t", [NOT, 128, FC, 512], BF16, kind="ExternalInput")
    wo_t = nc.dram_tensor("wo_t", [NOT, 128, HC, 512], BF16, kind="ExternalInput")
    bias_d = nc.dram_tensor("bias_bc", [128, D], F32, kind="ExternalInput")
    a_st_d = nc.dram_tensor("a_st", [128, 3, FC * 16], BF16, kind="ExternalInput")
    b_st_d = nc.dram_tensor("b_st", [128, 3, D // 8], BF16, kind="ExternalInput")
    lsi_d = nc.dram_tensor("lsi", [1, 4], I32, kind="ExternalInput")
    bidx_d = nc.dram_tensor("bidx", [1, 1], I32, kind="ExternalInput")
    ident_d = nc.dram_tensor("ident", [128, 128], BF16, kind="ExternalInput")
    onesb_d = nc.dram_tensor("onesb", [128, 1], BF16, kind="ExternalInput")
    onesrow_d = nc.dram_tensor("onesrow", [1, 128], F32, kind="ExternalInput")
    onesrowb_d = nc.dram_tensor("onesrowb", [1, 128], BF16, kind="ExternalInput")
    consts_d = nc.dram_tensor("constpool", [128, 8], F32, kind="ExternalInput")
    msel_d = nc.dram_tensor("msel", [128, SLOTS * 2 * 128], BF16,
                            kind="ExternalInput")
    out_rows = nc.dram_tensor("out_rows", [OWN, D], F32, kind="ExternalOutput")

    scratch_kind = "ExternalOutput" if DEBUG_SCRATCH else "Internal"
    kv_half = nc.dram_tensor("kv_half", [KVH, 128, S], BF16, kind="Internal")
    kv_full = nc.dram_tensor("kv_full", [2, KVH, 128, S], BF16, kind="Internal")
    Q_s = nc.dram_tensor("Q_s", [KVH, 128, SLOTS * G * 128], BF16, kind=scratch_kind)
    g_s = nc.dram_tensor("g_s", [OWN, D], F32, kind=scratch_kind)
    dbg1 = nc.dram_tensor("dbg1", [128, 512], F32, kind=scratch_kind) \
        if DEBUG_SCRATCH else None
    dbg_attn = nc.dram_tensor("dbg_attn", [128, HC * OWN], BF16, kind=scratch_kind) \
        if DEBUG_SCRATCH else None
    bbar_dram = nc.dram_tensor("bbar_dram", [16, D], BF16, kind="Internal")
    bc_dram = nc.dram_tensor("bc_dram", [3, 512], F32, kind="Internal")
    bc_dram2 = nc.dram_tensor("bc_dram2", [3, 512], F32, kind="Internal")

    es = ExitStack()
    sb = lambda name, shape, dt: es.enter_context(nc.sbuf_tensor(name, shape, dt))
    mksem = lambda name: es.enter_context(nc.semaphore(name))

    with es:
        s_ident = sb("s_ident", [128, 128], BF16)
        s_onesb = sb("s_onesb", [128, 1], BF16)
        s_onesrow = sb("s_onesrow", [1, 128], F32)
        s_Abar = sb("s_Abar", [128, FC * 16], BF16)
        s_onesrowb = sb("s_onesrowb", [1, 128], BF16)
        s_csel3b = sb("s_csel3b", [1, 4], BF16)
        s_const = sb("s_const", [128, 8], F32)
        s_lsi = sb("s_lsi", [1, 4], I32)
        s_bidx = sb("s_bidx", [1, 1], I32)
        s_small = sb("s_small", [1, 32], F32)
        s_csel3 = sb("s_csel3", [1, 4], F32)
        s_cselbc = sb("s_cselbc", [128, 4], F32)
        NCH = D // 512
        s_bns = sb("s_bns", [128, 2 * 6 * NCH], F32)
        s_mv = sb("s_mv", [128, 4], F32)
        s_std = sb("s_std", [128, 2], F32)
        s_rstd = sb("s_rstd", [128, 2], F32)

        sems = {k: mksem(k) for k in
                ("pe", "act", "dve", "pool", "qx0", "qx1", "qkv0", "qkv1",
                 "qw0", "qw1", "st0", "st1", "st2", "st3", "qld", "cc")}
        block = es.enter_context(nc.Block())
        p = Prog(nc, block, sems)
        A = lambda fn, **kw: p.add(DVE, fn, **kw)
        d = lambda eng, q, fn, waits=(): p.dma(eng, q, fn, waits)

        t_ident = d(SYNC, "qld", lambda e: e.dma_start(out=s_ident[:], in_=ident_d[:]))
        t_onesb = d(SYNC, "qld", lambda e: e.dma_start(out=s_onesb[:], in_=onesb_d[:]))
        t_onesrow = d(SYNC, "qld", lambda e: e.dma_start(out=s_onesrow[:], in_=onesrow_d[:]))
        t_onesrowb = d(SYNC, "qld", lambda e: e.dma_start(out=s_onesrowb[:], in_=onesrowb_d[:]))
        t_const = d(SYNC, "qld", lambda e: e.dma_start(out=s_const[:], in_=consts_d[:]))
        t_lsi = d(SYNC, "qld", lambda e: e.dma_start(out=s_lsi[:], in_=lsi_d[:]))
        t_bidx = d(SYNC, "qld", lambda e: e.dma_start(out=s_bidx[:], in_=bidx_d[:]))

        A(lambda e: e.tensor_copy(s_small[0:1, 8:9], s_bidx[:]),
          waits=[("qld", t_bidx)])
        A(lambda e: e.tensor_copy(s_small[0:1, 0:4], s_lsi[:]), drain=True)
        A(lambda e: e.tensor_scalar(s_small[0:1, 0:4], s_small[0:1, 0:4],
                                    s_small[0:1, 8:9], None, ALU.is_le), drain=True)
        A(lambda e: e.tensor_scalar(s_small[0:1, 9:10], s_small[0:1, 3:4],
                                    3.0, None, ALU.add), drain=True)
        A(lambda e: e.tensor_tensor(s_small[0:1, 9:10], s_small[0:1, 9:10],
                                    s_small[0:1, 2:3], ALU.mult), drain=True)
        A(lambda e: e.tensor_scalar(s_small[0:1, 10:11], s_small[0:1, 1:2],
                                    -1.0, 1.0, ALU.mult, ALU.add), drain=True)
        A(lambda e: e.tensor_tensor(s_small[0:1, 10:11], s_small[0:1, 10:11],
                                    s_small[0:1, 0:1], ALU.mult), drain=True)
        A(lambda e: e.scalar_tensor_tensor(s_small[0:1, 10:11], s_small[0:1, 1:2],
                                           2.0, s_small[0:1, 10:11],
                                           ALU.mult, ALU.add), drain=True)
        A(lambda e: e.tensor_scalar(s_small[0:1, 11:12], s_small[0:1, 2:3],
                                    -1.0, 1.0, ALU.mult, ALU.add), drain=True)
        A(lambda e: e.tensor_tensor(s_small[0:1, 10:11], s_small[0:1, 10:11],
                                    s_small[0:1, 11:12], ALU.mult), drain=True)
        A(lambda e: e.tensor_tensor(s_small[0:1, 12:13], s_small[0:1, 9:10],
                                    s_small[0:1, 10:11], ALU.add), drain=True)
        A(lambda e: e.tensor_scalar(s_small[0:1, 13:14], s_small[0:1, 12:13],
                                    2.0, None, ALU.is_ge), drain=True)
        A(lambda e: e.tensor_scalar(s_small[0:1, 14:15], s_small[0:1, 12:13],
                                    -2.0, None, ALU.add), drain=True)
        A(lambda e: e.tensor_scalar(s_small[0:1, 14:15], s_small[0:1, 14:15],
                                    0.0, 2.0, ALU.max, ALU.min), drain=True)
        for n in range(3):
            A(lambda e, n=n: e.tensor_scalar(s_small[0:1, 15 + n:16 + n],
                                             s_small[0:1, 14:15],
                                             float(n), None, ALU.is_equal),
              drain=True)
            tok_csel = A(lambda e, n=n: e.tensor_tensor(
                s_csel3[0:1, n:n + 1], s_small[0:1, 15 + n:16 + n],
                s_small[0:1, 13:14], ALU.mult), drain=True, drain_inc=True)

        p01 = ExitStack()
        p0 = ExitStack()
        with p0:
            p_csel = p0.enter_context(nc.psum_tensor("p_csel", [128, 4], F32))
            s_Ast = p01.enter_context(nc.sbuf_tensor("s_Ast", [128, 3, FC * 16], BF16))
            s_Bst = p01.enter_context(nc.sbuf_tensor("s_Bst", [128, 3, D // 8], BF16))
            s_Bpack = p01.enter_context(nc.sbuf_tensor("s_Bpack", [128, D // 8], BF16))

            tok_cselb = A(lambda e: e.tensor_copy(s_csel3b[0:1, 0:3],
                                                  s_csel3[0:1, 0:3]),
                          drain=True)
            tok_bc = p.add(PE, lambda e: e.matmul(
                p_csel[:, 0:3], s_onesrowb[:], s_csel3b[0:1, 0:3],
                start=True, stop=True),
                waits=[("dve", tok_cselb), ("qld", t_onesrowb)], inc=("pe", 1))
            tok_cselbc = p.add(ACT, lambda e: e.activation(
                s_cselbc[:, 0:3], p_csel[:, 0:3], AF.Copy, scale=1.0),
                waits=[("pe", tok_bc)], drain_inc=True)

            t_ast = d(SYNC, "qld", lambda e: e.dma_start(out=s_Ast[:], in_=a_st_d[:]))
            t_bst = d(SYNC, "qld", lambda e: e.dma_start(out=s_Bst[:], in_=b_st_d[:]))
            A(lambda e: e.tensor_scalar(s_Abar[:], s_Ast[:, 0, :],
                                        s_cselbc[:, 0:1], None, ALU.mult),
              waits=[("act", tok_cselbc), ("qld", t_bst)], drain=True)
            for n in (1,):
                A(lambda e, n=n: e.scalar_tensor_tensor(
                    s_Abar[:], s_Ast[:, n, :], s_cselbc[:, n:n + 1],
                    s_Abar[:], ALU.mult, ALU.add), drain=True)
            t_abar = A(lambda e: e.scalar_tensor_tensor(
                s_Abar[:], s_Ast[:, 2, :], s_cselbc[:, 2:3],
                s_Abar[:], ALU.mult, ALU.add), drain=True, drain_inc=True)
            A(lambda e: e.tensor_scalar(s_Bpack[:], s_Bst[:, 0, :],
                                        s_cselbc[:, 0:1], None, ALU.mult),
              drain=True)
            for n in (1,):
                A(lambda e, n=n: e.scalar_tensor_tensor(
                    s_Bpack[:], s_Bst[:, n, :], s_cselbc[:, n:n + 1],
                    s_Bpack[:], ALU.mult, ALU.add), drain=True)
            tok_bpack = A(lambda e: e.scalar_tensor_tensor(
                s_Bpack[:], s_Bst[:, 2, :], s_cselbc[:, 2:3],
                s_Bpack[:], ALU.mult, ALU.add), drain=True, drain_inc=True)
            bdst = bbar_dram[:].rearrange("r (ob j) -> (r ob) j", ob=8)
            t_bw = d(SYNC, "qld", lambda e: e.dma_start(out=bdst, in_=s_Bpack[:]),
                     waits=[("dve", tok_bpack)])
            p.flush()
        bar0 = p.barrier_waits()

        def ln_pipeline(kk, s_x_b, s_xn_b, extra_first_waits,
                        act_prev_tok, dma_tok, pe_guard=0, xsem="qx0",
                        xn_prev=0):
            i2 = kk % 2
            w0 = [(xsem, dma_tok)] + list(extra_first_waits)
            for cch in range(NCH):
                A(lambda e, cch=cch, i2=i2: e.bn_stats(
                    s_bns[:, (i2 * NCH + cch) * 6:(i2 * NCH + cch + 1) * 6],
                    s_x_b[:, cch * 512:(cch + 1) * 512]),
                  waits=w0 if cch == 0 else (), inc=None)
            tagg = A(lambda e, i2=i2: e.bn_aggr(
                s_mv[:, i2 * 2:i2 * 2 + 2],
                s_bns[:, i2 * NCH * 6:(i2 + 1) * NCH * 6]),
                waits=[("act", act_prev_tok)] if act_prev_tok else (),
                drain=True)
            a3 = p.add(ACT, lambda e, i2=i2: e.activation(
                s_std[:, i2:i2 + 1], s_mv[:, i2 * 2 + 1:i2 * 2 + 2],
                AF.Sqrt, bias=s_const[:, 0:1]),
                waits=[("dve", tagg), ("qld", t_const), ("dve", xn_prev)],
                drain_inc=True)
            A(lambda e, i2=i2: e.reciprocal(
                s_rstd[:, i2:i2 + 1], s_std[:, i2:i2 + 1]),
                waits=[("act", a3)], inc=None)
            w_xn = []
            if pe_guard:
                w_xn.append(("pe", pe_guard))
            t_xn = A(lambda e, i2=i2: e.tensor_scalar(
                s_xn_b[:], s_x_b[:], s_mv[:, i2 * 2:i2 * 2 + 1],
                s_rstd[:, i2:i2 + 1], ALU.subtract, ALU.mult),
                waits=w_xn, drain=True)
            return t_xn, a3

        p1 = ExitStack()
        with p1:
            s_wk = p1.enter_context(nc.sbuf_tensor("s_wk", [128, FC, DKV], BF16))
            s_wv = p1.enter_context(nc.sbuf_tensor("s_wv", [128, FC, DKV], BF16))
            s_x1 = [p1.enter_context(nc.sbuf_tensor(f"s_x1_{i}", [128, D], F32))
                    for i in range(2)]
            s_xnt1 = [p1.enter_context(nc.sbuf_tensor(f"s_xnt1_{i}", [128, D], BF16))
                      for i in range(2)]
            s_xnfm = p1.enter_context(nc.sbuf_tensor("s_xnfm", [128, FC, 256], BF16))
            s_kev = [p1.enter_context(nc.sbuf_tensor("s_kev0", [128, 256], BF16))] * 2
            s_vev = [p1.enter_context(nc.sbuf_tensor("s_vev0", [128, DKV], BF16))] * 2
            pp = ExitStack()
            p_tr = [pp.enter_context(nc.psum_tensor(f"p_tr{i}", [128, 512], F32))
                    for i in range(2)]
            p_K = [pp.enter_context(nc.psum_tensor(f"p_K{i}", [128, 512], F32))
                   for i in range(2)]
            p_V = [pp.enter_context(nc.psum_tensor(f"p_V{i}", [128, DKV], F32))
                   for i in range(2)]
            with pp:
                t_wk = d(POOL, "qw0", lambda e: e.dma_start(out=s_wk[:], in_=wk_t[:]))
                t_wv = d(POOL, "qw1", lambda e: e.dma_start(out=s_wv[:], in_=wv_t[:]))

                xn_done = {}
                evg_done = {}
                kev_dma = {}
                vev_dma = {}
                vtok_g = {}
                tr_last = {}
                a3_tok = {}
                txn_tok = {}
                NG = FC // 4
                NKK = NT1 * 2

                def p1_ln(kk):
                    i2 = kk % 2
                    tx = d(SYNC, f"qx{i2}", lambda e, kk=kk, i2=i2: e.dma_start(
                        out=s_x1[i2][:],
                        in_=x_half[kk * 128:(kk + 1) * 128, :]),
                        waits=[("dve", xn_done.get(kk - 2, 0))])
                    t_xn, a3 = ln_pipeline(kk, s_x1[i2],
                                           s_xnt1[i2], [],
                                           a3_tok.get(kk - 2, 0), tx,
                                           pe_guard=tr_last.get(kk - 2, 0),
                                           xsem=f"qx{i2}",
                                           xn_prev=xn_done.get(kk - 2, 0))
                    xn_done[kk] = t_xn
                    a3_tok[kk] = a3
                    txn_tok[kk] = t_xn

                def p1_tr(kk):
                    i2 = kk % 2
                    u = kk % 2
                    t_xn = txn_tok[kk]
                    for fcg in range(NG):
                        gg = kk * NG + fcg
                        gpar = gg % 2
                        tok_tr = None
                        for ii in range(4):
                            fc = fcg * 4 + ii
                            wts = [("dve", t_xn), ("qld", t_ident)]
                            if gg >= 2:
                                wts.append(("act", evg_done.get(gg - 2, 0)))
                            else:
                                wts.append(("act", tok_cselbc))
                            tok_tr = p.add(
                                PE, lambda e, fc=fc, ii=ii, gpar=gpar, i2=i2:
                                e.matmul(p_tr[gpar][:, ii * 128:(ii + 1) * 128],
                                         s_xnt1[i2][:, fc * 128:(fc + 1) * 128],
                                         s_ident[:], start=True, stop=True),
                                waits=wts if ii == 0 else (),
                                inc=("pe", 1) if ii == 3 else None)
                        evg_done[gg] = p.add(
                            ACT, lambda e, fcg=fcg, u=u, gpar=gpar: e.activation(
                                s_xnfm[:, fcg * 4:(fcg + 1) * 4,
                                       u * 128:(u + 1) * 128],
                                p_tr[gpar][:].rearrange("p (i j) -> p i j", i=4),
                                AF.Copy, scale=1.0),
                            waits=[("pe", tok_tr)])
                        tr_last[kk] = tok_tr

                p1_ln(0)
                p1_ln(1)
                for i1 in range(NT1):
                    p1_tr(i1 * 2)
                    p1_tr(i1 * 2 + 1)
                    if i1 * 2 + 2 < NKK:
                        p1_ln(i1 * 2 + 2)
                        p1_ln(i1 * 2 + 3)
                    ev_all = evg_done[((i1 * 2 + 1) * NG) + NG - 1]
                    tok_k = {}
                    for h in range(KVH):
                        hp = h % 2
                        wts = [("act", ev_all), ("qw0", t_wk),
                               ("act", tok_k.get(h - 2, 0))]
                        lastmm = None
                        for fc in range(FC):
                            lastmm = p.add(PE, lambda e, h=h, fc=fc, hp=hp: e.matmul(
                                p_K[hp][:, 0:256],
                                s_wk[:, fc, h * 128:(h + 1) * 128],
                                s_xnfm[:, fc, :],
                                start=(fc == 0), stop=(fc == FC - 1)),
                                waits=wts if fc == 0 else (),
                                inc=("pe", 1) if fc == FC - 1 else None)
                        tok_k[h] = p.add(ACT, lambda e, hp=hp: e.activation(
                            s_kev[hp][:], p_K[hp][:, 0:256], AF.Copy, scale=1.0),
                            waits=[("pe", lastmm), ("st0", kev_dma.get(0, 0))])
                        kev_dma[0] = d(SYNC, "st0",
                                       lambda e, h=h, i1=i1, hp=hp: e.dma_start(
                                           out=kv_half[h][:, i1 * 256:(i1 + 1) * 256],
                                           in_=s_kev[hp][:]),
                                       waits=[("act", tok_k[h])])
                    for u in range(2):
                        cglob = i1 * 2 + u
                        up = cglob % 2
                        wts = [("act", ev_all), ("qw1", t_wv),
                               ("act", vtok_g.get(cglob - 2, 0))]
                        lastmm = None
                        VW = min(512, DKV)
                        NH = DKV // VW
                        for fc in range(FC):
                            for half in range(NH):
                                lastmm = p.add(
                                    PE, lambda e, u=u, up=up, fc=fc, half=half,
                                    VW=VW: e.matmul(
                                        p_V[up][:, half * VW:(half + 1) * VW],
                                        s_xnfm[:, fc, u * 128:(u + 1) * 128],
                                        s_wv[:, fc, half * VW:(half + 1) * VW],
                                        start=(fc == 0), stop=(fc == FC - 1)),
                                    waits=wts if (fc == 0 and half == 0) else (),
                                    inc=("pe", 1) if (fc == FC - 1 and
                                                      half == NH - 1) else None)
                        tve = p.add(ACT, lambda e, up=up: e.activation(
                            s_vev[0][:], p_V[up][:], AF.Copy, scale=1.0),
                            waits=[("pe", lastmm), ("st2", vev_dma.get(0, 0))])
                        vtok_g[cglob] = tve
                        vev_dma[0] = d(SYNC, "st2",
                                       lambda e, cglob=cglob, up=up: e.dma_start(
                                           out=kv_half[:, :, S // 2 + cglob * 128:
                                                       S // 2 + (cglob + 1) * 128]
                                           .rearrange("h p d -> p h d"),
                                           in_=s_vev[0][:].rearrange(
                                               "p (h d) -> p h d", h=KVH)),
                                       waits=[("act", tve)])
                p.add(POOL, lambda e: e.collective_compute(
                    "AllGather", ALU.bypass,
                    replica_groups=[[0, 1], [2, 3], [4, 5], [6, 7]],
                    ins=[kv_half.ap().opt()],
                    outs=[kv_full.ap().opt()]),
                    waits=[("st0", p.n["st0"]), ("st2", p.n["st2"])],
                    inc=("cc", 1))
                p.flush()
        bar1 = p.barrier_waits()
        p01.close()

        p2 = ExitStack()
        with p2:
            s_xnown = p2.enter_context(nc.sbuf_tensor("s_xnown", [128, FC, OWN], BF16))
            s_UT = p2.enter_context(nc.sbuf_tensor("s_UT", [16, OWN], BF16))

            p2a = ExitStack()
            with p2a:
                s_x2 = [p2a.enter_context(nc.sbuf_tensor(f"s_x2_{i}", [128, D], F32))
                        for i in range(2)]
                s_xnt2 = [p2a.enter_context(
                    nc.sbuf_tensor(f"s_xnt2_{i}", [128, D], BF16)) for i in range(2)]
                pp2 = ExitStack()
                p_tr2 = [pp2.enter_context(
                    nc.psum_tensor(f"p_tr2{i}", [128, 512], F32)) for i in range(2)]
                with pp2:
                    evg_done = {}
                    a3_tok = {}
                    xn_done = {}
                    tr_last = {}
                    NG = FC // 4
                    for i1 in range(NT2):
                        for u in range(4):
                            kk = i1 * 4 + u
                            i2 = kk % 2
                            w = [("dve", xn_done.get(kk - 2, 0))]
                            if kk == 0:
                                w += bar1
                            tx = d(SYNC, f"qx{i2}", lambda e, i1=i1, u=u, i2=i2: e.dma_start(
                                out=s_x2[i2][:],
                                in_=x_own[i1 * 512 + u * 128: i1 * 512 + (u + 1) * 128, :]),
                                waits=w)
                            t_xn, a3 = ln_pipeline(kk, s_x2[i2],
                                                   s_xnt2[i2],
                                                   bar1 if kk == 0 else [],
                                                   a3_tok.get(kk - 2, 0), tx,
                                                   pe_guard=tr_last.get(kk - 2, 0),
                                                   xsem=f"qx{i2}",
                                                   xn_prev=xn_done.get(kk - 2, 0))
                            xn_done[kk] = t_xn
                            a3_tok[kk] = a3
                            for fcg in range(NG):
                                gg = kk * NG + fcg
                                gpar = gg % 2
                                tok_tr = None
                                for ii in range(4):
                                    fc = fcg * 4 + ii
                                    wts = [("dve", t_xn)]
                                    if gg >= 2:
                                        wts.append(("act", evg_done.get(gg - 2, 0)))
                                    elif kk == 0:
                                        wts += bar1
                                    tok_tr = p.add(
                                        PE, lambda e, fc=fc, ii=ii, gpar=gpar, i2=i2:
                                        e.matmul(p_tr2[gpar][:, ii * 128:(ii + 1) * 128],
                                                 s_xnt2[i2][:, fc * 128:(fc + 1) * 128],
                                                 s_ident[:], start=True, stop=True),
                                        waits=wts if ii == 0 else (),
                                        inc=("pe", 1) if ii == 3 else None)
                                tpos = i1 * 512 + u * 128
                                evg_done[gg] = p.add(
                                    ACT, lambda e, fcg=fcg, tpos=tpos, gpar=gpar:
                                    e.activation(
                                        s_xnown[:, fcg * 4:(fcg + 1) * 4,
                                                tpos:tpos + 128],
                                        p_tr2[gpar][:].rearrange("p (i j) -> p i j", i=4),
                                        AF.Copy, scale=1.0),
                                    waits=[("pe", tok_tr)])
                                tr_last[kk] = tok_tr
                    p.flush()
            xn_ready = p.n["act"]
            bar2a = p.barrier_waits()

            p2b = ExitStack()
            with p2b:
                s_wff = [p2b.enter_context(nc.sbuf_tensor(
                    f"s_wff{i}", [128, FC * 512], BF16)) for i in range(2)]
                s_bias = [p2b.enter_context(nc.sbuf_tensor(
                    "s_bias0", [128, 512], F32))] * 2
                s_gout = [p2b.enter_context(nc.sbuf_tensor(
                    f"s_g{i}", [128, 512], F32)) for i in range(2)]
                s_bb = [p2b.enter_context(nc.sbuf_tensor(
                    "s_bb0", [16, 512], BF16))] * 2
                s_dbgt = p2b.enter_context(nc.sbuf_tensor(
                    "s_dbgt", [128, 512], F32)) if DEBUG_SCRATCH else None
                pp3 = ExitStack()
                p_y = [pp3.enter_context(nc.psum_tensor(
                    f"p_y{i}", [128, 512], F32)) for i in range(2)]
                p_U = pp3.enter_context(nc.psum_tensor("p_U", [16, 512], F32))
                with pp3:
                    uev_tok = {}
                    t_ut = 0
                    for qt in range(OWN // 512):
                        lastmm = None
                        for fc in range(FC):
                            lastmm = p.add(PE, lambda e, qt=qt, fc=fc: e.matmul(
                                p_U[:, :], s_Abar[:, fc * 16:(fc + 1) * 16],
                                s_xnown[:, fc, qt * 512:(qt + 1) * 512],
                                start=(fc == 0), stop=(fc == FC - 1)),
                                waits=[("act", xn_ready), ("dve", t_abar),
                                       ("dve", uev_tok.get(qt - 1, 0))]
                                if fc == 0 else (),
                                inc=("pe", 1) if fc == FC - 1 else None)
                        uev_tok[qt] = A(lambda e, qt=qt: e.tensor_copy(
                            s_UT[:, qt * 512:(qt + 1) * 512], p_U[:, :]),
                            waits=[("pe", lastmm)], drain_inc=True)
                        t_ut = uev_tok[qt]

                    wff_tok = {}
                    bias_tok = {}
                    bb_tok = {}
                    pe_ot_done = {}
                    sg_done = {}
                    last_td = {}
                    gdma_tok = {}
                    silu_tok = {}
                    silu_last = {}
                    for ot in range(NOT):
                        op2 = ot % 2
                        wff_tok[ot] = d(POOL, f"qw{op2}", lambda e, ot=ot, op2=op2:
                                        e.dma_start(
                            out=s_wff[op2][:],
                            in_=wf_t[ot][:].rearrange("p a b -> p (a b)")),
                            waits=(list(bar2a) if ot == 0 else []) +
                                  [("act", silu_last.get(ot - 2, 0))])
                        bias_tok[ot] = d(POOL, f"qw{op2}", lambda e, ot=ot, op2=op2:
                                         e.dma_start(
                            out=s_bias[op2][:],
                            in_=bias_d[:, ot * 512:(ot + 1) * 512]),
                            waits=[("dve", last_td.get(ot - 1, 0))])
                        bb_tok[ot] = d(POOL, f"qw{op2}", lambda e, ot=ot, op2=op2:
                                       e.dma_start(
                            out=s_bb[op2][:],
                            in_=bbar_dram[:, ot * 512:(ot + 1) * 512]),
                            waits=([("qld", t_bw)] if ot < 2 else []) +
                                  [("dve", last_td.get(ot - 1, 0))])
                        for t in range(TCH):
                            tp = t % 2
                            kk = ot * TCH + t
                            wts = [(f"qw{op2}", bb_tok[ot]),
                                   ("dve", t_ut),
                                   ("act", silu_tok.get(kk - 2, 0))]
                            nolora = DEBUG_NO_LORA
                            for fc in range(FC):
                                lastmm = p.add(PE, lambda e, t=t, fc=fc, op2=op2, tp=tp,
                                               nolora=nolora: e.matmul(
                                    p_y[tp][:], s_xnown[:, fc, t * 128:(t + 1) * 128],
                                    s_wff[op2][:, fc * 512:(fc + 1) * 512],
                                    start=(fc == 0),
                                    stop=(nolora and fc == FC - 1)),
                                    waits=wts if fc == 0 else (),
                                    inc=("pe", 1) if (nolora and fc == FC - 1) else None)
                            if not nolora:
                                lastmm = p.add(PE, lambda e, t=t, ot=ot, tp=tp, op2=op2:
                                               e.matmul(
                                    p_y[tp][:], s_UT[:, t * 128:(t + 1) * 128],
                                    s_bb[op2][:],
                                    start=False, stop=True), inc=("pe", 1))
                            pe_ot_done[ot] = lastmm
                            td = A(lambda e, tp=tp, op2=op2: e.tensor_tensor(
                                p_y[tp][:], p_y[tp][:], s_bias[op2][:], ALU.add),
                                waits=[("pe", lastmm),
                                       (f"qw{op2}", bias_tok[ot])])
                            sg_done[kk] = td
                            last_td[ot] = td
                            ta = p.add(ACT, lambda e, tp=tp: e.activation(
                                s_gout[tp][:], p_y[tp][:], AF.Silu, scale=1.0),
                                waits=[("dve", td),
                                       (f"st{tp}", gdma_tok.get(kk - 2, 0))])
                            silu_tok[kk] = ta
                            silu_last[ot] = ta

                            gdma_tok[kk] = d(ACT, f"st{tp}",
                                             lambda e, t=t, ot=ot, tp=tp: e.dma_start(
                                out=g_s[t * 128:(t + 1) * 128,
                                        ot * 512:(ot + 1) * 512],
                                in_=s_gout[tp][:]), waits=[("act", ta)])
                    p.flush()
            bar2b = p.barrier_waits()

            p2c = ExitStack()
            with p2c:
                s_wq = [p2c.enter_context(nc.sbuf_tensor(
                    f"s_wq{i}", [128, FC * 128], BF16)) for i in range(2)]
                s_qev = [p2c.enter_context(nc.sbuf_tensor(
                    f"s_qev{i}", [128, 512], BF16)) for i in range(2)]
                pp4 = ExitStack()
                p_q = [pp4.enter_context(nc.psum_tensor(
                    f"p_q{i}", [128, 512], F32)) for i in range(2)]
                with pp4:
                    wq_tok = {}
                    qev_tok = {}
                    qev_last = {}
                    qdma_tok = {}
                    pe_m_done = {}
                    NQT = OWN // 512
                    for m in range(HC):
                        mp = m % 2
                        wq_tok[m] = d(POOL, f"qw{mp}", lambda e, m=m, mp=mp: e.dma_start(
                            out=s_wq[mp][:],
                            in_=wq_t[m][:].rearrange("p a b -> p (a b)")),
                            waits=(list(bar2b) if m == 0 else []) +
                                  [("dve", qev_last.get(m - 2, 0))])
                        kvh, g = m // G, m % G
                        for qt in range(NQT):
                            iq = m * NQT + qt
                            qp = iq % 2
                            wts = [(f"qw{mp}", wq_tok[m]), ("act", xn_ready),
                                   ("dve", qev_tok.get(iq - 2, 0))]
                            if m == 0 and qt == 0:
                                wts = wts + list(bar2b)
                            lastmm = None
                            for fc in range(FC):
                                lastmm = p.add(
                                    PE, lambda e, m=m, qt=qt, fc=fc, qp=qp, mp=mp:
                                    e.matmul(
                                        p_q[qp][:],
                                        s_wq[mp][:, fc * 128:(fc + 1) * 128],
                                        s_xnown[:, fc, qt * 512:(qt + 1) * 512],
                                        start=(fc == 0), stop=(fc == FC - 1)),
                                    waits=wts if fc == 0 else (),
                                    inc=("pe", 1) if fc == FC - 1 else None)
                            pe_m_done[m] = lastmm
                            tqe = A(lambda e, qp=qp: e.tensor_copy(
                                s_qev[qp][:], p_q[qp][:]),
                                waits=[("pe", lastmm),
                                       (f"st{2 + qp}", qdma_tok.get(qp, 0))])
                            qev_tok[iq] = tqe
                            qev_last[m] = tqe
                            qdma_tok[qp] = d(SYNC, f"st{2 + qp}",
                                             lambda e, kvh=kvh, g=g, qt=qt, qp=qp:
                                             e.dma_start(
                                out=Q_s[kvh].rearrange(
                                    "p (slot g q) -> p slot g q",
                                    g=G, q=128)[:, qt * 4:(qt + 1) * 4, g, :],
                                in_=s_qev[qp][:].rearrange(
                                    "p (jj q) -> p jj q", jj=4)),
                                waits=[("dve", tqe)])
                    p.flush()
        bar2 = p.barrier_waits()

        p3 = ExitStack()
        with p3:
            s_attn = p3.enter_context(nc.sbuf_tensor("s_attn", [128, HC, OWN], BF16))

            p3a = ExitStack()
            with p3a:
                s_masks = p3a.enter_context(nc.sbuf_tensor(
                    "s_masks", [128, SLOTS * 2 * 128], BF16))
                s_K = [p3a.enter_context(nc.sbuf_tensor(
                    f"s_K{i}", [128, S], BF16)) for i in range(2)]
                s_V = [p3a.enter_context(nc.sbuf_tensor(
                    f"s_V{i}", [128, S], BF16)) for i in range(2)]
                s_Q = [p3a.enter_context(nc.sbuf_tensor(
                    f"s_Q{i}", [128, SLOTS * G * 128 // 2], BF16))
                    for i in range(2)]
                s_P = [p3a.enter_context(nc.sbuf_tensor(
                    f"s_P{i}", [128, 1024], BF16)) for i in range(4)]
                s_l = [p3a.enter_context(nc.sbuf_tensor(
                    f"s_l{i}", [1, 512], F32)) for i in range(3)]
                s_lT = [p3a.enter_context(nc.sbuf_tensor(
                    f"s_lT{i}", [128, 4], F32)) for i in range(3)]
                s_rT = [p3a.enter_context(nc.sbuf_tensor(
                    f"s_rT{i}", [128, 4], F32)) for i in range(3)]

                s_bc = [p3a.enter_context(nc.sbuf_tensor(
                    f"s_bc{i}", [128, 512], F32)) for i in range(3)]
                pp5 = ExitStack()
                p_s = [pp5.enter_context(nc.psum_tensor(
                    f"p_s{i}", [128, 1024], F32)) for i in range(3)]
                p_at = pp5.enter_context(nc.psum_tensor("p_at0", [128, 512], F32))
                p_lb = pp5.enter_context(nc.psum_tensor("p_lb", [128, 512], F32))
                p_l = [p_lb[0:1, :], p_lb[32:33, :], p_lb[64:65, :]]
                with pp5:
                    kv_load = {}
                    qh_tok = {}
                    qk_half_last = {}
                    t_masks = None
                    cg = 0
                    gu = 0
                    exp_tok = {}
                    mask_tok = {}
                    stt_tok = {}
                    bcev_tok = {}
                    lfin_tok = {}
                    evict_tok = {}
                    backlog = []
                    pend_epi = []
                    pend_b = []
                    pend_c = []
                    epi_tr = {}
                    epi_w = {}
                    rcp_tok = {}
                    t3_tok = {}

                    t2_tok = {}
                    t4_tok = {}

                    def emit_epi_a(gu_, kvh_, slot_, up_):
                        lcp = A(lambda e, up=up_, gu=gu_: e.tensor_copy(
                            s_l[up][:], p_l[gu % 3]),
                            waits=[("pe", lfin_tok[gu_]),
                                   ("st0", epi_w.get(gu_ - 3, 0))])
                        epi_tr[gu_] = lcp
                        t1 = d(SYNC, "st0", lambda e, up=up_: e.dma_start(
                            out=bc_dram[up:up + 1, :], in_=s_l[up][:]),
                            waits=[("dve", lcp)])
                        epi_w[gu_] = t1
                        t2_tok[gu_] = d(SYNC, "st1", lambda e, up=up_: e.dma_start(
                            out=s_lT[up][:],
                            in_=bass.AP(bc_dram, up * 512, [[4, 128], [1, 4]])),
                            waits=[("st0", t1),
                                   ("dve", rcp_tok.get(gu_ - 3, 0))])

                    def emit_epi_b(gu_, kvh_, slot_, up_):
                        r = A(lambda e, up=up_: e.reciprocal(
                            s_rT[up][:], s_lT[up][:]),
                            waits=[("st1", t2_tok[gu_]),
                                   ("st0", t3_tok.get(gu_ - 3, 0))])
                        rcp_tok[gu_] = r
                        t3 = d(SYNC, "st0", lambda e, up=up_: e.dma_start(
                            out=bass.AP(bc_dram2, up * 512, [[4, 128], [1, 4]]),
                            in_=s_rT[up][:]),
                            waits=[("dve", r)])
                        t3_tok[gu_] = t3
                        t4_tok[gu_] = d(SYNC, "st1", lambda e, up=up_: e.dma_start(
                            out=s_bc[up][:],
                            in_=bass.AP(bc_dram2, up * 512, [[0, 128], [1, 512]])),
                            waits=[("st0", t3),
                                   ("dve", stt_tok.get(gu_ - 3, 0))])

                    def emit_epi_c(gu_, kvh_, slot_, up_):
                        stt_tok[gu_] = A(lambda e, up=up_, kvh=kvh_, slot=slot_:
                                         e.scalar_tensor_tensor(
                            s_attn[:, kvh * G:(kvh + 1) * G,
                                   slot * 128:(slot + 1) * 128],
                            s_attn[:, kvh * G:(kvh + 1) * G,
                                   slot * 128:(slot + 1) * 128],
                            1.0,
                            s_bc[up][:].rearrange("p (i j) -> p i j", i=G),
                            ALU.bypass, ALU.mult),
                            waits=[("st1", t4_tok[gu_]),
                                   ("dve", evict_tok[gu_])])

                    def emit_rest(fp_, kvh_, kp_, slot_, pj_, C_, gu_, up_,
                                  ms_, msp_, tqk_):
                        ps = fp_ % 3
                        sp = fp_ % 4
                        cc0 = 2 * pj_
                        te = p.add(ACT, lambda e, ps=ps, sp=sp: e.activation(
                            s_P[sp][:], p_s[ps][:], AF.Exp, scale=ISCALE),
                            waits=[("pe", tqk_)])
                        exp_tok[fp_] = te
                        dep = ("act", te)
                        if cc0 == C_ - 2:
                            tm = None
                            for gg_ in range(G):
                                tm = A(lambda e, sp=sp, gg_=gg_, slot=slot_:
                                       e.tensor_tensor(
                                    s_P[sp][:].rearrange(
                                        "p (c gq) -> p c gq", c=2)
                                    [:, :, gg_ * 128:(gg_ + 1) * 128],
                                    s_P[sp][:].rearrange(
                                        "p (c gq) -> p c gq", c=2)
                                    [:, :, gg_ * 128:(gg_ + 1) * 128],
                                    s_masks[:, slot * 256:(slot + 1) * 256]
                                    .rearrange("p (c q) -> p c q", c=2),
                                    ALU.mult),
                                    waits=[("act", te), ("qld", t_masks)]
                                    if gg_ == 0 else (),
                                    inc="auto" if gg_ == G - 1 else None)
                            mask_tok[fp_] = tm
                            dep = ("dve", tm)
                        for h_ in range(2):
                            cc = cc0 + h_
                            aw = [dep] if h_ == 0 else []
                            if cc == 0:
                                aw.append(("dve", evict_tok.get(gu_ - 1, 0)))
                            p.add(PE, lambda e, cc=cc, sp=sp, kp=kp_,
                                  h_=h_: e.matmul(
                                p_at[:], s_V[kp][:, cc * 128:(cc + 1) * 128],
                                s_P[sp][:, h_ * 512:(h_ + 1) * 512],
                                start=(cc == 0), stop=(cc == C_ - 1),
                                skip_group_check=True),
                                waits=aw)
                        for h_ in range(2):
                            cc = cc0 + h_
                            ll = p.add(PE, lambda e, cc=cc, sp=sp, gu=gu_,
                                       h_=h_: e.matmul(
                                p_l[gu % 3], s_onesb[:],
                                s_P[sp][:, h_ * 512:(h_ + 1) * 512],
                                start=(cc == 0), stop=(cc == C_ - 1),
                                skip_group_check=True),
                                waits=[("dve", epi_tr.get(gu_ - 3, 0))]
                                if cc == 0 else (),
                                inc=("pe", 1) if cc == C_ - 1 else None)
                        if cc0 == C_ - 2:
                            lfin_tok[gu_] = ll
                            evict_tok[gu_] = A(
                                lambda e, kvh=kvh_, slot=slot_:
                                e.tensor_copy(
                                    s_attn[:, kvh * G:(kvh + 1) * G,
                                           slot * 128:(slot + 1) * 128],
                                    p_at[:].rearrange(
                                        "p (i j) -> p i j", i=G)),
                                waits=[("pe", ll)])
                            pend_epi.append((gu_, kvh_, slot_, up_, fp_))

                    fp = 0
                    for kvh in range(KVH):
                        kp = kvh % 2
                        wts = list(bar2) if kvh == 0 else \
                            [("pe", lfin_tok.get((kvh - 1) * SLOTS - 1, 0))]
                        d(POOL, f"qkv{kp}", lambda e, kvh=kvh, kp=kp: e.dma_start(
                            out=s_K[kp][:, 0:S // 2],
                            in_=kv_full[0, kvh, :, 0:S // 2]),
                            waits=wts + [("cc", 1)])
                        d(POOL, f"qkv{kp}", lambda e, kvh=kvh, kp=kp: e.dma_start(
                            out=s_K[kp][:, S // 2:S],
                            in_=kv_full[1, kvh, :, 0:S // 2]))
                        d(POOL, f"qkv{kp}", lambda e, kvh=kvh, kp=kp: e.dma_start(
                            out=s_V[kp][:, 0:S // 2],
                            in_=kv_full[0, kvh, :, S // 2:S]))
                        d(POOL, f"qkv{kp}", lambda e, kvh=kvh, kp=kp: e.dma_start(
                            out=s_V[kp][:, S // 2:S],
                            in_=kv_full[1, kvh, :, S // 2:S]))
                        QH = SLOTS * G * 128 // 2
                        for h2 in (1, 0):
                            qh_tok[(kvh, h2)] = d(
                                POOL, f"qkv{kp}",
                                lambda e, kvh=kvh, h2=h2, QH=QH: e.dma_start(
                                    out=s_Q[h2][:],
                                    in_=Q_s[kvh][:, h2 * QH:(h2 + 1) * QH]),
                                waits=[("pe", qk_half_last.get(
                                    (kvh - 1, h2), 0))])
                        if t_masks is None:
                            t_masks = d(SYNC, "qld", lambda e: e.dma_start(
                                out=s_masks[:], in_=msel_d[:]), waits=bar2)
                        slot_order = []
                        for half in (1, 0):
                            lo = half * (SLOTS // 2)
                            hi = lo + SLOTS // 2 - 1
                            while lo <= hi:
                                slot_order.append(hi)
                                if lo < hi:
                                    slot_order.append(lo)
                                lo, hi = lo + 1, hi - 1
                        for slot in slot_order:
                            C = 2 * (slot + 1)
                            ms = kvh * SLOTS + slot
                            for pj in range(C // 2):
                                ps = fp % 3
                                up = gu % 3
                                while pend_epi and fp - pend_epi[0][4] >= 1:
                                    e_ = pend_epi.pop(0)
                                    emit_epi_a(*e_[:4])
                                    pend_b.append(e_)
                                while pend_b and fp - pend_b[0][4] >= 4:
                                    e_ = pend_b.pop(0)
                                    emit_epi_b(*e_[:4])
                                    pend_c.append(e_)
                                while pend_c and fp - pend_c[0][4] >= 8:
                                    emit_epi_c(*pend_c.pop(0)[:4])
                                if len(backlog) >= 3:
                                    emit_rest(*backlog.pop(0))
                                wts = [("act", exp_tok.get(fp - 3, 0)),
                                       (f"qkv{kp}",
                                        qh_tok[(kvh, slot // (SLOTS // 2))])]
                                if (fp - 4) in mask_tok:
                                    wts.append(("dve", mask_tok[fp - 4]))
                                tqk = None
                                for h_ in range(2):
                                    cc = 2 * pj + h_
                                    tqk = p.add(PE, lambda e, cc=cc, slot=slot,
                                                ps=ps, kp=kp, h_=h_: e.matmul(
                                        p_s[ps][:, h_ * 512:(h_ + 1) * 512],
                                        s_K[kp][:, cc * 128:(cc + 1) * 128],
                                        s_Q[slot // (SLOTS // 2)]
                                        [:, (slot % (SLOTS // 2)) * 512:
                                         (slot % (SLOTS // 2) + 1) * 512],
                                        start=True, stop=True,
                                        skip_group_check=True),
                                        waits=wts if h_ == 0 else (),
                                        inc=("pe", 1) if h_ == 1 else None)
                                qk_half_last[(kvh, slot // (SLOTS // 2))] = tqk
                                backlog.append((fp, kvh, kp, slot, pj, C, gu,
                                                up, ms, 0, tqk))
                                fp += 1
                            gu += 1
                    while backlog:
                        emit_rest(*backlog.pop(0))
                    while pend_epi:
                        e_ = pend_epi.pop(0)
                        emit_epi_a(*e_[:4])
                        pend_b.append(e_)
                    while pend_b:
                        e_ = pend_b.pop(0)
                        emit_epi_b(*e_[:4])
                        pend_c.append(e_)
                    while pend_c:
                        emit_epi_c(*pend_c.pop(0)[:4])
                    attn_done = p.n["dve"]
                    if DEBUG_SCRATCH:
                        d(SYNC, "qld", lambda e: e.dma_start(
                            out=dbg_attn[:],
                            in_=s_attn[:].rearrange("p a b -> p (a b)")),
                            waits=[("dve", attn_done)])
                    p.flush()
            bar3 = p.barrier_waits()

            p4 = ExitStack()
            with p4:
                s_wo = [p4.enter_context(nc.sbuf_tensor(
                    f"s_wo{i}", [128, HC * 512], BF16)) for i in range(2)]
                s_g4 = [p4.enter_context(nc.sbuf_tensor(
                    f"s_g4_{i}", [128, 512], F32)) for i in range(2)]
                s_x4 = [p4.enter_context(nc.sbuf_tensor(
                    f"s_x4_{i}", [128, 512], F32)) for i in range(2)]
                s_o4 = [p4.enter_context(nc.sbuf_tensor(
                    f"s_o4_{i}", [128, 512], F32)) for i in range(2)]
                pp6 = ExitStack()
                p_y4 = [pp6.enter_context(nc.psum_tensor(
                    f"p_y4{i}", [128, 512], F32)) for i in range(2)]
                with pp6:
                    wo_tok = {}
                    pe_ot_done = {}
                    sg_tok = {}
                    out_tok = {}
                    add_tok = {}
                    add_last = {}
                    for ot in range(NOT):
                        op2 = ot % 2
                        wo_tok[ot] = d(POOL, f"qw{op2}", lambda e, ot=ot, op2=op2:
                                       e.dma_start(
                            out=s_wo[op2][:],
                            in_=wo_t[ot][:].rearrange("p a b -> p (a b)")),
                            waits=(list(bar3) if ot == 0 else []) +
                                  [("dve", add_last.get(ot - 2, 0))])
                        for t in range(TCH):
                            tp = t % 2
                            kk = ot * TCH + t
                            tg = d(SYNC, f"st{tp}", lambda e, t=t, ot=ot, tp=tp:
                                   e.dma_start(
                                out=s_g4[tp][:],
                                in_=g_s[t * 128:(t + 1) * 128,
                                        ot * 512:(ot + 1) * 512]),
                                waits=(list(bar3) if kk == 0 else []) +
                                      [("dve", sg_tok.get(kk - 2, 0))])
                            txl = d(SYNC, f"st{tp}", lambda e, t=t, ot=ot, tp=tp:
                                    e.dma_start(
                                out=s_x4[tp][:],
                                in_=x_own[t * 128:(t + 1) * 128,
                                          ot * 512:(ot + 1) * 512]),
                                waits=[("dve", add_tok.get(kk - 2, 0))])
                            wts = [(f"qw{op2}", wo_tok[ot]), ("dve", attn_done),
                                   ("dve", add_tok.get(kk - 2, 0))]
                            if kk == 0:
                                wts = wts + list(bar3)
                            lastmm = None
                            for fc in range(HC):
                                lastmm = p.add(
                                    PE, lambda e, t=t, fc=fc, op2=op2, tp=tp:
                                    e.matmul(
                                        p_y4[tp][:],
                                        s_attn[:, fc, t * 128:(t + 1) * 128],
                                        s_wo[op2][:, fc * 512:(fc + 1) * 512],
                                        start=(fc == 0), stop=(fc == HC - 1)),
                                    waits=wts if fc == 0 else (),
                                    inc=("pe", 1) if fc == HC - 1 else None)
                            pe_ot_done[ot] = lastmm
                            tsg = A(lambda e, tp=tp: e.tensor_tensor(
                                p_y4[tp][:], p_y4[tp][:], s_g4[tp][:], ALU.mult),
                                waits=[("pe", lastmm), (f"st{tp}", txl)])
                            sg_tok[kk] = tsg
                            tadd = A(lambda e, tp=tp: e.tensor_tensor(
                                s_o4[tp][:], p_y4[tp][:], s_x4[tp][:], ALU.add),
                                waits=[(f"st{tp}", txl),
                                       (f"st{2 + tp}", out_tok.get(kk - 2, 0))],
                                drain=True)
                            add_tok[kk] = tadd
                            add_last[ot] = tadd
                            out_tok[kk] = d(ACT, f"st{2 + tp}",
                                            lambda e, t=t, ot=ot, tp=tp: e.dma_start(
                                out=out_rows[t * 128:(t + 1) * 128,
                                             ot * 512:(ot + 1) * 512],
                                in_=s_o4[tp][:]), waits=[("dve", tadd)])
                    p.add(ACT, lambda e: e.nop(),
                          waits=[("st2", p.n["st2"]), ("st3", p.n["st3"])],
                          inc=None)
                    p.flush()
    return nc


def _stage_weights(w_q, w_k, w_v, w_o, W, bias, A, B, cfg):
    c = cfg_derived(cfg)
    D, FC, NOT, HC, DKV = c["D"], c["FC"], c["NOT"], c["HC"], c["DKV"]
    f32 = np.float32
    wk_t = np.ascontiguousarray(w_k.reshape(FC, 128, DKV).transpose(1, 0, 2))
    wv_t = np.ascontiguousarray(w_v.reshape(FC, 128, DKV).transpose(1, 0, 2))
    wq_t = np.ascontiguousarray(
        w_q.reshape(FC, 128, HC, 128).transpose(2, 1, 0, 3))
    wf_t = np.ascontiguousarray(
        W.reshape(NOT, 512, FC, 128).transpose(0, 3, 2, 1))
    wo_t = np.ascontiguousarray(
        w_o.reshape(HC, 128, NOT, 512).transpose(2, 1, 0, 3))
    bias_bc = np.ascontiguousarray(np.broadcast_to(bias.astype(f32), (128, D)))
    a_st = np.ascontiguousarray(
        A.reshape(3, 16, FC, 128).transpose(3, 0, 2, 1).reshape(128, 3, FC * 16))
    Do8 = D // 8
    p_idx = np.arange(128)
    r_idx = p_idx // 8
    ob_idx = p_idx % 8
    b_st = np.stack([B[n][(ob_idx[:, None] * Do8 + np.arange(Do8)[None, :]),
                          r_idx[:, None]] for n in range(3)], axis=1)
    bf16 = ml_dtypes.bfloat16
    return dict(wk_t=wk_t.astype(bf16),
                wv_t=wv_t.astype(bf16),
                wq_t=wq_t.astype(bf16),
                wf_t=wf_t.astype(bf16),
                wo_t=wo_t.astype(bf16),
                bias_bc=bias_bc, a_st=a_st.astype(bf16),
                b_st=np.ascontiguousarray(b_st).astype(bf16))


def _stage_masks(parity, cfg):
    c = cfg_derived(cfg)
    SLOTS, NBLK, G = c["SLOTS"], c["NBLK"], c["G"]
    blocks = own_blocks(parity, NBLK)
    tri = (np.arange(128)[:, None] <= np.arange(128)[None, :]).astype(np.float32)
    ones = np.ones((128, 128), np.float32)
    zeros = np.zeros((128, 128), np.float32)
    m = np.empty((SLOTS, 2, 128, 128), np.float32)
    for s in range(SLOTS):
        n = blocks[s] + 1
        C = 2 * (s + 1)
        if n == C:
            m[s, 0] = ones
            m[s, 1] = tri
        else:
            assert n == C - 1
            m[s, 0] = tri
            m[s, 1] = zeros
    m = m.transpose(2, 0, 1, 3).reshape(128, SLOTS * 2 * 128)
    return np.ascontiguousarray(m).astype(ml_dtypes.bfloat16)


def kernel(L0_attn_hidden_states, L0_attn_w_q, L0_attn_w_k, L0_attn_w_v,
           L0_attn_w_o, L0_ffn_weight, L0_ffn_bias, L0_ffn_lora_A,
           L0_ffn_lora_B, L0_ffn_lora_start_indices, _cfg=None):
    cfg = _cfg or REAL_CFG
    c = cfg_derived(cfg)
    B, S, D, NBLK = c["B"], c["S"], c["D"], c["NBLK"]
    x = np.asarray(L0_attn_hidden_states, dtype=np.float32)
    shared = _stage_weights(np.asarray(L0_attn_w_q, np.float32),
                            np.asarray(L0_attn_w_k, np.float32),
                            np.asarray(L0_attn_w_v, np.float32),
                            np.asarray(L0_attn_w_o, np.float32),
                            np.asarray(L0_ffn_weight, np.float32),
                            np.asarray(L0_ffn_bias, np.float32),
                            np.asarray(L0_ffn_lora_A, np.float32),
                            np.asarray(L0_ffn_lora_B, np.float32), cfg)
    lsi = np.asarray(L0_ffn_lora_start_indices, np.int32).reshape(1, 4)
    ident = np.eye(128, dtype=ml_dtypes.bfloat16)
    onesb = np.ones((128, 1), dtype=ml_dtypes.bfloat16)
    onesrow = np.ones((1, 128), dtype=np.float32)
    onesrowb = np.ones((1, 128), dtype=ml_dtypes.bfloat16)
    consts = np.zeros((128, 8), np.float32)
    consts[:, 0] = c["EPS"]
    masks = {par: _stage_masks(par, cfg) for par in (0, 1)}

    n_cores = 2 * B
    in_maps = []
    for core in range(n_cores):
        b, par = core // 2, core % 2
        blocks = own_blocks(par, NBLK)
        perm = np.concatenate([np.arange(j * 128, (j + 1) * 128) for j in blocks])
        m = dict(shared)
        m["x_half"] = np.ascontiguousarray(
            x[b][par * (S // 2):(par + 1) * (S // 2)])
        m["x_own"] = np.ascontiguousarray(x[b][perm])
        m["lsi"] = lsi
        m["bidx"] = np.array([[b]], dtype=np.int32)
        m["ident"] = ident
        m["onesb"] = onesb
        m["onesrow"] = onesrow
        m["onesrowb"] = onesrowb
        m["constpool"] = consts
        m["msel"] = masks[par]
        in_maps.append(m)

    nc = build_nc(cfg)
    res = run_bass_kernel_spmd(nc, in_maps, core_ids=list(range(n_cores)))
    out = np.empty((B, S, D), dtype=np.float32)
    for core in range(n_cores):
        b, par = core // 2, core % 2
        rows = np.asarray(res.results[core]["out_rows"])
        blocks = own_blocks(par, NBLK)
        for s_i, j in enumerate(blocks):
            out[b, j * 128:(j + 1) * 128] = rows[s_i * 128:(s_i + 1) * 128]
    return out


# version 32
# speedup vs baseline: 1.0118x; 1.0118x over previous
import numpy as np
import ml_dtypes
from contextlib import ExitStack

import concourse.bass as bass
import concourse.mybir as mybir
from concourse.bass_utils import run_bass_kernel_spmd

F32 = mybir.dt.float32
BF16 = mybir.dt.bfloat16
I32 = mybir.dt.int32
AF = mybir.ActivationFunctionType
ALU = mybir.AluOpType

PE, ACT, DVE, POOL, SYNC = "pe", "act", "dve", "pool", "sync"
ENG_SEM = {PE: "pe", ACT: "act", DVE: "dve", POOL: "pool"}

REAL_CFG = dict(D=4096, S=4096, KVH=8, G=4, B=4, EPS=1e-5)
DEBUG_SCRATCH = False
DEBUG_NO_LORA = False
DEBUG_P3_SERIAL = False


def cfg_derived(cfg):
    c = dict(cfg)
    c["FC"] = c["D"] // 128
    c["NBLK"] = c["S"] // 128
    c["SLOTS"] = c["NBLK"] // 2
    c["OWN"] = c["S"] // 2
    c["DKV"] = c["KVH"] * 128
    c["DQ"] = c["KVH"] * c["G"] * 128
    c["HC"] = c["DQ"] // 128
    assert c["DQ"] == c["D"], "design assumes DQ == D"
    c["NT1"] = c["S"] // 512
    c["NT2"] = c["OWN"] // 512
    c["TCH"] = c["OWN"] // 128
    c["NOT"] = c["D"] // 512
    return c


def own_blocks(parity, nblk):
    keep = (0, 3) if parity == 0 else (1, 2)
    return [j for j in range(nblk) if j % 4 in keep]


class Prog:

    def __init__(self, nc, block, sems):
        self.nc = nc
        self.block = block
        self.sems = sems
        self.L = {k: [] for k in (PE, ACT, DVE, POOL, SYNC)}
        self.n = {"pe": 0, "act": 0, "dve": 0, "pool": 0,
                  "qx0": 0, "qx1": 0, "qkv0": 0, "qkv1": 0,
                  "qw0": 0, "qw1": 0, "st0": 0, "st1": 0,
                  "st2": 0, "st3": 0, "qld": 0, "cc": 0}
        self.lastw = {k: {} for k in (PE, ACT, DVE, POOL, SYNC)}

    def add(self, eng, fn, waits=(), inc="auto", drain=False, drain_inc=False):
        if inc == "auto":
            inc = (ENG_SEM[eng], 1) if eng in ENG_SEM else None
        if inc is not None and eng in (ACT, DVE, POOL) and inc[0] == ENG_SEM[eng]:
            drain_inc = True
        tok = None
        if inc is not None:
            self.n[inc[0]] += inc[1]
            tok = self.n[inc[0]]
        self.L[eng].append((fn, tuple(waits), inc, drain, drain_inc))
        return tok

    def dma(self, eng, q, fn, waits=()):
        w = list(waits) + [(q, self.n[q])]
        return self.add(eng, fn, waits=w, inc=(q, 16))

    def barrier_waits(self):
        return [(k, v) for k, v in self.n.items() if v > 0 and k != "cc"]

    def flush(self):
        sems = self.sems

        def runner(lst, engkey):
            last = self.lastw[engkey]

            def _(e):
                for fn, waits, inc, drain, drain_inc in lst:
                    for sk, v in waits:
                        if v <= 0:
                            continue
                        if engkey in ENG_SEM and sk == ENG_SEM[engkey]:
                            continue
                        if last.get(sk, -1) >= v:
                            continue
                        e.wait_ge(sems[sk], v)
                        last[sk] = v
                    if drain:
                        e.drain()
                    inst = fn(e)
                    if inc is not None:
                        if drain_inc:
                            e.drain().then_inc(sems[inc[0]], inc[1])
                        else:
                            inst.then_inc(sems[inc[0]], inc[1])
            return _

        if self.L[PE]:
            self.block.tensor(runner(self.L[PE], PE))
        if self.L[ACT]:
            self.block.scalar(runner(self.L[ACT], ACT))
        if self.L[DVE]:
            self.block.vector(runner(self.L[DVE], DVE))
        if self.L[POOL]:
            self.block.gpsimd(runner(self.L[POOL], POOL))
        if self.L[SYNC]:
            self.block.sync(runner(self.L[SYNC], SYNC))
        self.L = {k: [] for k in (PE, ACT, DVE, POOL, SYNC)}


def build_nc(cfg):
    c = cfg_derived(cfg)
    D, S, KVH, G = c["D"], c["S"], c["KVH"], c["G"]
    FC, NBLK, SLOTS, OWN = c["FC"], c["NBLK"], c["SLOTS"], c["OWN"]
    DKV, HC = c["DKV"], c["HC"]
    NT1, NT2, TCH, NOT = c["NT1"], c["NT2"], c["TCH"], c["NOT"]
    EPS = c["EPS"]
    ISCALE = float(1.0 / np.sqrt(128.0))
    invD = 1.0 / D
    invD2 = invD * invD

    nc = bass.Bass("TRN2", target_bir_lowering=False, debug=False)

    x_half = nc.dram_tensor("x_half", [S // 2, D], F32, kind="ExternalInput")
    x_own = nc.dram_tensor("x_own", [OWN, D], F32, kind="ExternalInput")
    wk_t = nc.dram_tensor("wk_t", [128, FC, DKV], BF16, kind="ExternalInput")
    wv_t = nc.dram_tensor("wv_t", [128, FC, DKV], BF16, kind="ExternalInput")
    wq_t = nc.dram_tensor("wq_t", [HC, 128, FC, 128], BF16, kind="ExternalInput")
    wf_t = nc.dram_tensor("wf_t", [NOT, 128, FC, 512], BF16, kind="ExternalInput")
    wo_t = nc.dram_tensor("wo_t", [NOT, 128, HC, 512], BF16, kind="ExternalInput")
    bias_d = nc.dram_tensor("bias_bc", [128, D], F32, kind="ExternalInput")
    a_st_d = nc.dram_tensor("a_st", [128, 3, FC * 16], BF16, kind="ExternalInput")
    b_st_d = nc.dram_tensor("b_st", [128, 3, D // 8], BF16, kind="ExternalInput")
    lsi_d = nc.dram_tensor("lsi", [1, 4], I32, kind="ExternalInput")
    bidx_d = nc.dram_tensor("bidx", [1, 1], I32, kind="ExternalInput")
    ident_d = nc.dram_tensor("ident", [128, 128], BF16, kind="ExternalInput")
    onesb_d = nc.dram_tensor("onesb", [128, 1], BF16, kind="ExternalInput")
    onesrow_d = nc.dram_tensor("onesrow", [1, 128], F32, kind="ExternalInput")
    onesrowb_d = nc.dram_tensor("onesrowb", [1, 128], BF16, kind="ExternalInput")
    consts_d = nc.dram_tensor("constpool", [128, 8], F32, kind="ExternalInput")
    msel_d = nc.dram_tensor("msel", [128, SLOTS * 2 * 128], BF16,
                            kind="ExternalInput")
    out_rows = nc.dram_tensor("out_rows", [OWN, D], F32, kind="ExternalOutput")

    scratch_kind = "ExternalOutput" if DEBUG_SCRATCH else "Internal"
    kv_half = nc.dram_tensor("kv_half", [KVH, 128, S], BF16, kind="Internal")
    kv_full = nc.dram_tensor("kv_full", [2, KVH, 128, S], BF16, kind="Internal")
    Q_s = nc.dram_tensor("Q_s", [KVH, 128, SLOTS * G * 128], BF16, kind=scratch_kind)
    g_s = nc.dram_tensor("g_s", [OWN, D], F32, kind=scratch_kind)
    dbg1 = nc.dram_tensor("dbg1", [128, 512], F32, kind=scratch_kind) \
        if DEBUG_SCRATCH else None
    dbg_attn = nc.dram_tensor("dbg_attn", [128, HC * OWN], BF16, kind=scratch_kind) \
        if DEBUG_SCRATCH else None
    bbar_dram = nc.dram_tensor("bbar_dram", [16, D], BF16, kind="Internal")
    bc_dram = nc.dram_tensor("bc_dram", [3, 512], F32, kind="Internal")
    bc_dram2 = nc.dram_tensor("bc_dram2", [3, 512], F32, kind="Internal")

    es = ExitStack()
    sb = lambda name, shape, dt: es.enter_context(nc.sbuf_tensor(name, shape, dt))
    mksem = lambda name: es.enter_context(nc.semaphore(name))

    with es:
        s_ident = sb("s_ident", [128, 128], BF16)
        s_onesb = sb("s_onesb", [128, 1], BF16)
        s_onesrow = sb("s_onesrow", [1, 128], F32)
        s_Abar = sb("s_Abar", [128, FC * 16], BF16)
        s_onesrowb = sb("s_onesrowb", [1, 128], BF16)
        s_csel3b = sb("s_csel3b", [1, 4], BF16)
        s_const = sb("s_const", [128, 8], F32)
        s_lsi = sb("s_lsi", [1, 4], I32)
        s_bidx = sb("s_bidx", [1, 1], I32)
        s_small = sb("s_small", [1, 32], F32)
        s_csel3 = sb("s_csel3", [1, 4], F32)
        s_cselbc = sb("s_cselbc", [128, 4], F32)
        NCH = D // 512
        s_bns = sb("s_bns", [128, 2 * 6 * NCH], F32)
        s_mv = sb("s_mv", [128, 4], F32)
        s_std = sb("s_std", [128, 2], F32)
        s_rstd = sb("s_rstd", [128, 2], F32)

        sems = {k: mksem(k) for k in
                ("pe", "act", "dve", "pool", "qx0", "qx1", "qkv0", "qkv1",
                 "qw0", "qw1", "st0", "st1", "st2", "st3", "qld", "cc")}
        block = es.enter_context(nc.Block())
        p = Prog(nc, block, sems)
        A = lambda fn, **kw: p.add(DVE, fn, **kw)
        d = lambda eng, q, fn, waits=(): p.dma(eng, q, fn, waits)

        t_ident = d(SYNC, "qld", lambda e: e.dma_start(out=s_ident[:], in_=ident_d[:]))
        t_onesb = d(SYNC, "qld", lambda e: e.dma_start(out=s_onesb[:], in_=onesb_d[:]))
        t_onesrow = d(SYNC, "qld", lambda e: e.dma_start(out=s_onesrow[:], in_=onesrow_d[:]))
        t_onesrowb = d(SYNC, "qld", lambda e: e.dma_start(out=s_onesrowb[:], in_=onesrowb_d[:]))
        t_const = d(SYNC, "qld", lambda e: e.dma_start(out=s_const[:], in_=consts_d[:]))
        t_lsi = d(SYNC, "qld", lambda e: e.dma_start(out=s_lsi[:], in_=lsi_d[:]))
        t_bidx = d(SYNC, "qld", lambda e: e.dma_start(out=s_bidx[:], in_=bidx_d[:]))

        A(lambda e: e.tensor_copy(s_small[0:1, 8:9], s_bidx[:]),
          waits=[("qld", t_bidx)])
        A(lambda e: e.tensor_copy(s_small[0:1, 0:4], s_lsi[:]), drain=True)
        A(lambda e: e.tensor_scalar(s_small[0:1, 0:4], s_small[0:1, 0:4],
                                    s_small[0:1, 8:9], None, ALU.is_le), drain=True)
        A(lambda e: e.tensor_scalar(s_small[0:1, 9:10], s_small[0:1, 3:4],
                                    3.0, None, ALU.add), drain=True)
        A(lambda e: e.tensor_tensor(s_small[0:1, 9:10], s_small[0:1, 9:10],
                                    s_small[0:1, 2:3], ALU.mult), drain=True)
        A(lambda e: e.tensor_scalar(s_small[0:1, 10:11], s_small[0:1, 1:2],
                                    -1.0, 1.0, ALU.mult, ALU.add), drain=True)
        A(lambda e: e.tensor_tensor(s_small[0:1, 10:11], s_small[0:1, 10:11],
                                    s_small[0:1, 0:1], ALU.mult), drain=True)
        A(lambda e: e.scalar_tensor_tensor(s_small[0:1, 10:11], s_small[0:1, 1:2],
                                           2.0, s_small[0:1, 10:11],
                                           ALU.mult, ALU.add), drain=True)
        A(lambda e: e.tensor_scalar(s_small[0:1, 11:12], s_small[0:1, 2:3],
                                    -1.0, 1.0, ALU.mult, ALU.add), drain=True)
        A(lambda e: e.tensor_tensor(s_small[0:1, 10:11], s_small[0:1, 10:11],
                                    s_small[0:1, 11:12], ALU.mult), drain=True)
        A(lambda e: e.tensor_tensor(s_small[0:1, 12:13], s_small[0:1, 9:10],
                                    s_small[0:1, 10:11], ALU.add), drain=True)
        A(lambda e: e.tensor_scalar(s_small[0:1, 13:14], s_small[0:1, 12:13],
                                    2.0, None, ALU.is_ge), drain=True)
        A(lambda e: e.tensor_scalar(s_small[0:1, 14:15], s_small[0:1, 12:13],
                                    -2.0, None, ALU.add), drain=True)
        A(lambda e: e.tensor_scalar(s_small[0:1, 14:15], s_small[0:1, 14:15],
                                    0.0, 2.0, ALU.max, ALU.min), drain=True)
        for n in range(3):
            A(lambda e, n=n: e.tensor_scalar(s_small[0:1, 15 + n:16 + n],
                                             s_small[0:1, 14:15],
                                             float(n), None, ALU.is_equal),
              drain=True)
            tok_csel = A(lambda e, n=n: e.tensor_tensor(
                s_csel3[0:1, n:n + 1], s_small[0:1, 15 + n:16 + n],
                s_small[0:1, 13:14], ALU.mult), drain=True, drain_inc=True)

        p01 = ExitStack()
        p0 = ExitStack()
        with p0:
            p_csel = p0.enter_context(nc.psum_tensor("p_csel", [128, 4], F32))
            s_Ast = p01.enter_context(nc.sbuf_tensor("s_Ast", [128, 3, FC * 16], BF16))
            s_Bst = p01.enter_context(nc.sbuf_tensor("s_Bst", [128, 3, D // 8], BF16))
            s_Bpack = p01.enter_context(nc.sbuf_tensor("s_Bpack", [128, D // 8], BF16))

            tok_cselb = A(lambda e: e.tensor_copy(s_csel3b[0:1, 0:3],
                                                  s_csel3[0:1, 0:3]),
                          drain=True)
            tok_bc = p.add(PE, lambda e: e.matmul(
                p_csel[:, 0:3], s_onesrowb[:], s_csel3b[0:1, 0:3],
                start=True, stop=True),
                waits=[("dve", tok_cselb), ("qld", t_onesrowb)], inc=("pe", 1))
            tok_cselbc = p.add(ACT, lambda e: e.activation(
                s_cselbc[:, 0:3], p_csel[:, 0:3], AF.Copy, scale=1.0),
                waits=[("pe", tok_bc)], drain_inc=True)

            t_ast = d(SYNC, "qld", lambda e: e.dma_start(out=s_Ast[:], in_=a_st_d[:]))
            t_bst = d(SYNC, "qld", lambda e: e.dma_start(out=s_Bst[:], in_=b_st_d[:]))
            A(lambda e: e.tensor_scalar(s_Abar[:], s_Ast[:, 0, :],
                                        s_cselbc[:, 0:1], None, ALU.mult),
              waits=[("act", tok_cselbc), ("qld", t_bst)], drain=True)
            for n in (1,):
                A(lambda e, n=n: e.scalar_tensor_tensor(
                    s_Abar[:], s_Ast[:, n, :], s_cselbc[:, n:n + 1],
                    s_Abar[:], ALU.mult, ALU.add), drain=True)
            t_abar = A(lambda e: e.scalar_tensor_tensor(
                s_Abar[:], s_Ast[:, 2, :], s_cselbc[:, 2:3],
                s_Abar[:], ALU.mult, ALU.add), drain=True, drain_inc=True)
            A(lambda e: e.tensor_scalar(s_Bpack[:], s_Bst[:, 0, :],
                                        s_cselbc[:, 0:1], None, ALU.mult),
              drain=True)
            for n in (1,):
                A(lambda e, n=n: e.scalar_tensor_tensor(
                    s_Bpack[:], s_Bst[:, n, :], s_cselbc[:, n:n + 1],
                    s_Bpack[:], ALU.mult, ALU.add), drain=True)
            tok_bpack = A(lambda e: e.scalar_tensor_tensor(
                s_Bpack[:], s_Bst[:, 2, :], s_cselbc[:, 2:3],
                s_Bpack[:], ALU.mult, ALU.add), drain=True, drain_inc=True)
            bdst = bbar_dram[:].rearrange("r (ob j) -> (r ob) j", ob=8)
            t_bw = d(SYNC, "qld", lambda e: e.dma_start(out=bdst, in_=s_Bpack[:]),
                     waits=[("dve", tok_bpack)])
            p.flush()
        bar0 = p.barrier_waits()

        def ln_pipeline(kk, s_x_b, s_xn_b, extra_first_waits,
                        act_prev_tok, dma_tok, pe_guard=0, xsem="qx0",
                        xn_prev=0):
            i2 = kk % 2
            w0 = [(xsem, dma_tok)] + list(extra_first_waits)
            for cch in range(NCH):
                A(lambda e, cch=cch, i2=i2: e.bn_stats(
                    s_bns[:, (i2 * NCH + cch) * 6:(i2 * NCH + cch + 1) * 6],
                    s_x_b[:, cch * 512:(cch + 1) * 512]),
                  waits=w0 if cch == 0 else (), inc=None)
            tagg = A(lambda e, i2=i2: e.bn_aggr(
                s_mv[:, i2 * 2:i2 * 2 + 2],
                s_bns[:, i2 * NCH * 6:(i2 + 1) * NCH * 6]),
                waits=[("act", act_prev_tok)] if act_prev_tok else (),
                drain=True)
            a3 = p.add(ACT, lambda e, i2=i2: e.activation(
                s_std[:, i2:i2 + 1], s_mv[:, i2 * 2 + 1:i2 * 2 + 2],
                AF.Sqrt, bias=s_const[:, 0:1]),
                waits=[("dve", tagg), ("qld", t_const), ("dve", xn_prev)],
                drain_inc=True)
            A(lambda e, i2=i2: e.reciprocal(
                s_rstd[:, i2:i2 + 1], s_std[:, i2:i2 + 1]),
                waits=[("act", a3)], inc=None)
            w_xn = []
            if pe_guard:
                w_xn.append(("pe", pe_guard))
            t_xn = A(lambda e, i2=i2: e.tensor_scalar(
                s_xn_b[:], s_x_b[:], s_mv[:, i2 * 2:i2 * 2 + 1],
                s_rstd[:, i2:i2 + 1], ALU.subtract, ALU.mult),
                waits=w_xn, drain=True)
            return t_xn, a3

        p1 = ExitStack()
        with p1:
            s_wk = p1.enter_context(nc.sbuf_tensor("s_wk", [128, FC, DKV], BF16))
            s_wv = p1.enter_context(nc.sbuf_tensor("s_wv", [128, FC, DKV], BF16))
            s_x1 = [p1.enter_context(nc.sbuf_tensor(f"s_x1_{i}", [128, D], F32))
                    for i in range(2)]
            s_xnt1 = [p1.enter_context(nc.sbuf_tensor(f"s_xnt1_{i}", [128, D], BF16))
                      for i in range(2)]
            s_xnfm = p1.enter_context(nc.sbuf_tensor("s_xnfm", [128, FC, 256], BF16))
            s_kev = [p1.enter_context(nc.sbuf_tensor("s_kev0", [128, 256], BF16))] * 2
            s_vev = [p1.enter_context(nc.sbuf_tensor("s_vev0", [128, DKV], BF16))] * 2
            pp = ExitStack()
            p_tr = [pp.enter_context(nc.psum_tensor(f"p_tr{i}", [128, 512], F32))
                    for i in range(2)]
            p_K = [pp.enter_context(nc.psum_tensor(f"p_K{i}", [128, 512], F32))
                   for i in range(2)]
            p_V = [pp.enter_context(nc.psum_tensor(f"p_V{i}", [128, DKV], F32))
                   for i in range(2)]
            with pp:
                t_wk = d(POOL, "qw0", lambda e: e.dma_start(out=s_wk[:], in_=wk_t[:]))
                t_wv = d(POOL, "qw1", lambda e: e.dma_start(out=s_wv[:], in_=wv_t[:]))

                xn_done = {}
                evg_done = {}
                kev_dma = {}
                vev_dma = {}
                vtok_g = {}
                tr_last = {}
                a3_tok = {}
                txn_tok = {}
                NG = FC // 4
                NKK = NT1 * 2

                def p1_ln(kk):
                    i2 = kk % 2
                    tx = d(SYNC, f"qx{i2}", lambda e, kk=kk, i2=i2: e.dma_start(
                        out=s_x1[i2][:],
                        in_=x_half[kk * 128:(kk + 1) * 128, :]),
                        waits=[("dve", xn_done.get(kk - 2, 0))])
                    t_xn, a3 = ln_pipeline(kk, s_x1[i2],
                                           s_xnt1[i2], [],
                                           a3_tok.get(kk - 2, 0), tx,
                                           pe_guard=tr_last.get(kk - 2, 0),
                                           xsem=f"qx{i2}",
                                           xn_prev=xn_done.get(kk - 2, 0))
                    xn_done[kk] = t_xn
                    a3_tok[kk] = a3
                    txn_tok[kk] = t_xn

                def p1_tr(kk):
                    i2 = kk % 2
                    u = kk % 2
                    t_xn = txn_tok[kk]
                    for fcg in range(NG):
                        gg = kk * NG + fcg
                        gpar = gg % 2
                        tok_tr = None
                        for ii in range(4):
                            fc = fcg * 4 + ii
                            wts = [("dve", t_xn), ("qld", t_ident)]
                            if gg >= 2:
                                wts.append(("act", evg_done.get(gg - 2, 0)))
                            else:
                                wts.append(("act", tok_cselbc))
                            tok_tr = p.add(
                                PE, lambda e, fc=fc, ii=ii, gpar=gpar, i2=i2:
                                e.matmul(p_tr[gpar][:, ii * 128:(ii + 1) * 128],
                                         s_xnt1[i2][:, fc * 128:(fc + 1) * 128],
                                         s_ident[:], start=True, stop=True),
                                waits=wts if ii == 0 else (),
                                inc=("pe", 1) if ii == 3 else None)
                        evg_done[gg] = p.add(
                            ACT, lambda e, fcg=fcg, u=u, gpar=gpar: e.activation(
                                s_xnfm[:, fcg * 4:(fcg + 1) * 4,
                                       u * 128:(u + 1) * 128],
                                p_tr[gpar][:].rearrange("p (i j) -> p i j", i=4),
                                AF.Copy, scale=1.0),
                            waits=[("pe", tok_tr)])
                        tr_last[kk] = tok_tr

                p1_ln(0)
                p1_ln(1)
                for i1 in range(NT1):
                    p1_tr(i1 * 2)
                    p1_tr(i1 * 2 + 1)
                    if i1 * 2 + 2 < NKK:
                        p1_ln(i1 * 2 + 2)
                        p1_ln(i1 * 2 + 3)
                    ev_all = evg_done[((i1 * 2 + 1) * NG) + NG - 1]
                    tok_k = {}
                    for h in range(KVH):
                        hp = h % 2
                        wts = [("act", ev_all), ("qw0", t_wk),
                               ("act", tok_k.get(h - 2, 0))]
                        lastmm = None
                        for fc in range(FC):
                            lastmm = p.add(PE, lambda e, h=h, fc=fc, hp=hp: e.matmul(
                                p_K[hp][:, 0:256],
                                s_wk[:, fc, h * 128:(h + 1) * 128],
                                s_xnfm[:, fc, :],
                                start=(fc == 0), stop=(fc == FC - 1)),
                                waits=wts if fc == 0 else (),
                                inc=("pe", 1) if fc == FC - 1 else None)
                        tok_k[h] = p.add(ACT, lambda e, hp=hp: e.activation(
                            s_kev[hp][:], p_K[hp][:, 0:256], AF.Copy, scale=1.0),
                            waits=[("pe", lastmm), ("st0", kev_dma.get(0, 0))])
                        kev_dma[0] = d(SYNC, "st0",
                                       lambda e, h=h, i1=i1, hp=hp: e.dma_start(
                                           out=kv_half[h][:, i1 * 256:(i1 + 1) * 256],
                                           in_=s_kev[hp][:]),
                                       waits=[("act", tok_k[h])])
                    for u in range(2):
                        cglob = i1 * 2 + u
                        up = cglob % 2
                        wts = [("act", ev_all), ("qw1", t_wv),
                               ("act", vtok_g.get(cglob - 2, 0))]
                        lastmm = None
                        VW = min(512, DKV)
                        NH = DKV // VW
                        for fc in range(FC):
                            for half in range(NH):
                                lastmm = p.add(
                                    PE, lambda e, u=u, up=up, fc=fc, half=half,
                                    VW=VW: e.matmul(
                                        p_V[up][:, half * VW:(half + 1) * VW],
                                        s_xnfm[:, fc, u * 128:(u + 1) * 128],
                                        s_wv[:, fc, half * VW:(half + 1) * VW],
                                        start=(fc == 0), stop=(fc == FC - 1)),
                                    waits=wts if (fc == 0 and half == 0) else (),
                                    inc=("pe", 1) if (fc == FC - 1 and
                                                      half == NH - 1) else None)
                        tve = p.add(ACT, lambda e, up=up: e.activation(
                            s_vev[0][:], p_V[up][:], AF.Copy, scale=1.0),
                            waits=[("pe", lastmm), ("st2", vev_dma.get(0, 0))])
                        vtok_g[cglob] = tve
                        vev_dma[0] = d(SYNC, "st2",
                                       lambda e, cglob=cglob, up=up: e.dma_start(
                                           out=kv_half[:, :, S // 2 + cglob * 128:
                                                       S // 2 + (cglob + 1) * 128]
                                           .rearrange("h p d -> p h d"),
                                           in_=s_vev[0][:].rearrange(
                                               "p (h d) -> p h d", h=KVH)),
                                       waits=[("act", tve)])
                p.add(POOL, lambda e: e.collective_compute(
                    "AllGather", ALU.bypass,
                    replica_groups=[[0, 1], [2, 3], [4, 5], [6, 7]],
                    ins=[kv_half.ap().opt()],
                    outs=[kv_full.ap().opt()]),
                    waits=[("st0", p.n["st0"]), ("st2", p.n["st2"])],
                    inc=("cc", 1))
                p.flush()
        bar1 = p.barrier_waits()
        p01.close()

        p2 = ExitStack()
        with p2:
            s_xnown = p2.enter_context(nc.sbuf_tensor("s_xnown", [128, FC, OWN], BF16))
            s_UT = p2.enter_context(nc.sbuf_tensor("s_UT", [16, OWN], BF16))

            p2a = ExitStack()
            with p2a:
                s_x2 = [p2a.enter_context(nc.sbuf_tensor(f"s_x2_{i}", [128, D], F32))
                        for i in range(2)]
                s_xnt2 = [p2a.enter_context(
                    nc.sbuf_tensor(f"s_xnt2_{i}", [128, D], BF16)) for i in range(2)]
                pp2 = ExitStack()
                p_tr2 = [pp2.enter_context(
                    nc.psum_tensor(f"p_tr2{i}", [128, 512], F32)) for i in range(2)]
                with pp2:
                    evg_done = {}
                    a3_tok = {}
                    xn_done = {}
                    tr_last = {}
                    NG = FC // 4
                    NKK2 = NT2 * 4

                    def p2a_ln(kk):
                        i2 = kk % 2
                        w = [("dve", xn_done.get(kk - 2, 0))]
                        if kk == 0:
                            w += bar1
                        tx = d(SYNC, f"qx{i2}", lambda e, kk=kk, i2=i2: e.dma_start(
                            out=s_x2[i2][:],
                            in_=x_own[kk * 128:(kk + 1) * 128, :]),
                            waits=w)
                        t_xn, a3 = ln_pipeline(kk, s_x2[i2],
                                               s_xnt2[i2],
                                               bar1 if kk == 0 else [],
                                               a3_tok.get(kk - 2, 0), tx,
                                               pe_guard=tr_last.get(kk - 2, 0),
                                               xsem=f"qx{i2}",
                                               xn_prev=xn_done.get(kk - 2, 0))
                        xn_done[kk] = t_xn
                        a3_tok[kk] = a3

                    def p2a_tr(kk):
                        i2 = kk % 2
                        t_xn = xn_done[kk]
                        for fcg in range(NG):
                            gg = kk * NG + fcg
                            gpar = gg % 2
                            tok_tr = None
                            for ii in range(4):
                                fc = fcg * 4 + ii
                                wts = [("dve", t_xn)]
                                if gg >= 2:
                                    wts.append(("act", evg_done.get(gg - 2, 0)))
                                elif kk == 0:
                                    wts += bar1
                                tok_tr = p.add(
                                    PE, lambda e, fc=fc, ii=ii, gpar=gpar, i2=i2:
                                    e.matmul(p_tr2[gpar][:, ii * 128:(ii + 1) * 128],
                                             s_xnt2[i2][:, fc * 128:(fc + 1) * 128],
                                             s_ident[:], start=True, stop=True),
                                    waits=wts if ii == 0 else (),
                                    inc=("pe", 1) if ii == 3 else None)
                            tpos = kk * 128
                            evg_done[gg] = p.add(
                                ACT, lambda e, fcg=fcg, tpos=tpos, gpar=gpar:
                                e.activation(
                                    s_xnown[:, fcg * 4:(fcg + 1) * 4,
                                            tpos:tpos + 128],
                                    p_tr2[gpar][:].rearrange("p (i j) -> p i j", i=4),
                                    AF.Copy, scale=1.0),
                                waits=[("pe", tok_tr)])
                            tr_last[kk] = tok_tr

                    p2a_ln(0)
                    p2a_ln(1)
                    for kk in range(NKK2):
                        p2a_tr(kk)
                        if kk + 2 < NKK2:
                            p2a_ln(kk + 2)
                    p.flush()
            xn_ready = p.n["act"]
            bar2a = p.barrier_waits()

            p2b = ExitStack()
            with p2b:
                s_wff = [p2b.enter_context(nc.sbuf_tensor(
                    f"s_wff{i}", [128, FC * 512], BF16)) for i in range(2)]
                s_bias = [p2b.enter_context(nc.sbuf_tensor(
                    "s_bias0", [128, 512], F32))] * 2
                s_gout = [p2b.enter_context(nc.sbuf_tensor(
                    f"s_g{i}", [128, 512], F32)) for i in range(2)]
                s_bb = [p2b.enter_context(nc.sbuf_tensor(
                    "s_bb0", [16, 512], BF16))] * 2
                s_dbgt = p2b.enter_context(nc.sbuf_tensor(
                    "s_dbgt", [128, 512], F32)) if DEBUG_SCRATCH else None
                pp3 = ExitStack()
                p_y = [pp3.enter_context(nc.psum_tensor(
                    f"p_y{i}", [128, 512], F32)) for i in range(2)]
                p_U = pp3.enter_context(nc.psum_tensor("p_U", [16, 512], F32))
                with pp3:
                    uev_tok = {}
                    t_ut = 0
                    for qt in range(OWN // 512):
                        lastmm = None
                        for fc in range(FC):
                            lastmm = p.add(PE, lambda e, qt=qt, fc=fc: e.matmul(
                                p_U[:, :], s_Abar[:, fc * 16:(fc + 1) * 16],
                                s_xnown[:, fc, qt * 512:(qt + 1) * 512],
                                start=(fc == 0), stop=(fc == FC - 1)),
                                waits=[("act", xn_ready), ("dve", t_abar),
                                       ("dve", uev_tok.get(qt - 1, 0))]
                                if fc == 0 else (),
                                inc=("pe", 1) if fc == FC - 1 else None)
                        uev_tok[qt] = A(lambda e, qt=qt: e.tensor_copy(
                            s_UT[:, qt * 512:(qt + 1) * 512], p_U[:, :]),
                            waits=[("pe", lastmm)], drain_inc=True)
                        t_ut = uev_tok[qt]

                    wff_tok = {}
                    bias_tok = {}
                    bb_tok = {}
                    pe_ot_done = {}
                    sg_done = {}
                    last_td = {}
                    gdma_tok = {}
                    silu_tok = {}
                    silu_last = {}
                    for ot in range(NOT):
                        op2 = ot % 2
                        wff_tok[ot] = d(POOL, f"qw{op2}", lambda e, ot=ot, op2=op2:
                                        e.dma_start(
                            out=s_wff[op2][:],
                            in_=wf_t[ot][:].rearrange("p a b -> p (a b)")),
                            waits=(list(bar2a) if ot == 0 else []) +
                                  [("act", silu_last.get(ot - 2, 0))])
                        bias_tok[ot] = d(POOL, f"qw{op2}", lambda e, ot=ot, op2=op2:
                                         e.dma_start(
                            out=s_bias[op2][:],
                            in_=bias_d[:, ot * 512:(ot + 1) * 512]),
                            waits=[("dve", last_td.get(ot - 1, 0))])
                        bb_tok[ot] = d(POOL, f"qw{op2}", lambda e, ot=ot, op2=op2:
                                       e.dma_start(
                            out=s_bb[op2][:],
                            in_=bbar_dram[:, ot * 512:(ot + 1) * 512]),
                            waits=([("qld", t_bw)] if ot < 2 else []) +
                                  [("dve", last_td.get(ot - 1, 0))])
                        for t in range(TCH):
                            tp = t % 2
                            kk = ot * TCH + t
                            wts = [(f"qw{op2}", wff_tok[ot]),
                                   ("dve", t_ut),
                                   ("act", silu_tok.get(kk - 2, 0))]
                            nolora = DEBUG_NO_LORA
                            for fc in range(FC):
                                lastmm = p.add(PE, lambda e, t=t, fc=fc, op2=op2, tp=tp,
                                               nolora=nolora: e.matmul(
                                    p_y[tp][:], s_xnown[:, fc, t * 128:(t + 1) * 128],
                                    s_wff[op2][:, fc * 512:(fc + 1) * 512],
                                    start=(fc == 0),
                                    stop=(nolora and fc == FC - 1)),
                                    waits=wts if fc == 0 else (),
                                    inc=("pe", 1) if (nolora and fc == FC - 1) else None)
                            if not nolora:
                                lastmm = p.add(PE, lambda e, t=t, ot=ot, tp=tp, op2=op2:
                                               e.matmul(
                                    p_y[tp][:], s_UT[:, t * 128:(t + 1) * 128],
                                    s_bb[op2][:],
                                    start=False, stop=True),
                                    waits=[(f"qw{op2}", bb_tok[ot])]
                                    if t == 0 else (),
                                    inc=("pe", 1))
                            pe_ot_done[ot] = lastmm
                            td = A(lambda e, tp=tp, op2=op2: e.tensor_tensor(
                                p_y[tp][:], p_y[tp][:], s_bias[op2][:], ALU.add),
                                waits=[("pe", lastmm),
                                       (f"qw{op2}", bias_tok[ot])])
                            sg_done[kk] = td
                            last_td[ot] = td
                            ta = p.add(ACT, lambda e, tp=tp: e.activation(
                                s_gout[tp][:], p_y[tp][:], AF.Silu, scale=1.0),
                                waits=[("dve", td),
                                       (f"st{tp}", gdma_tok.get(kk - 2, 0))])
                            silu_tok[kk] = ta
                            silu_last[ot] = ta

                            gdma_tok[kk] = d(ACT, f"st{tp}",
                                             lambda e, t=t, ot=ot, tp=tp: e.dma_start(
                                out=g_s[t * 128:(t + 1) * 128,
                                        ot * 512:(ot + 1) * 512],
                                in_=s_gout[tp][:]), waits=[("act", ta)])
                    p.flush()
            bar2b = p.barrier_waits()

            p2c = ExitStack()
            with p2c:
                s_wq = [p2c.enter_context(nc.sbuf_tensor(
                    f"s_wq{i}", [128, FC * 128], BF16)) for i in range(2)]
                s_qev = [p2c.enter_context(nc.sbuf_tensor(
                    f"s_qev{i}", [128, 512], BF16)) for i in range(2)]
                pp4 = ExitStack()
                p_q = [pp4.enter_context(nc.psum_tensor(
                    f"p_q{i}", [128, 512], F32)) for i in range(2)]
                with pp4:
                    wq_tok = {}
                    qev_tok = {}
                    qev_last = {}
                    qdma_tok = {}
                    pe_m_done = {}
                    NQT = OWN // 512
                    for m in range(HC):
                        mp = m % 2
                        wq_tok[m] = d(POOL, f"qw{mp}", lambda e, m=m, mp=mp: e.dma_start(
                            out=s_wq[mp][:],
                            in_=wq_t[m][:].rearrange("p a b -> p (a b)")),
                            waits=(list(bar2b) if m == 0 else []) +
                                  [("dve", qev_last.get(m - 2, 0))])
                        kvh, g = m // G, m % G
                        for qt in range(NQT):
                            iq = m * NQT + qt
                            qp = iq % 2
                            wts = [(f"qw{mp}", wq_tok[m]), ("act", xn_ready),
                                   ("dve", qev_tok.get(iq - 2, 0))]
                            if m == 0 and qt == 0:
                                wts = wts + list(bar2b)
                            lastmm = None
                            for fc in range(FC):
                                lastmm = p.add(
                                    PE, lambda e, m=m, qt=qt, fc=fc, qp=qp, mp=mp:
                                    e.matmul(
                                        p_q[qp][:],
                                        s_wq[mp][:, fc * 128:(fc + 1) * 128],
                                        s_xnown[:, fc, qt * 512:(qt + 1) * 512],
                                        start=(fc == 0), stop=(fc == FC - 1)),
                                    waits=wts if fc == 0 else (),
                                    inc=("pe", 1) if fc == FC - 1 else None)
                            pe_m_done[m] = lastmm
                            tqe = A(lambda e, qp=qp: e.tensor_copy(
                                s_qev[qp][:], p_q[qp][:]),
                                waits=[("pe", lastmm),
                                       (f"st{2 + qp}", qdma_tok.get(qp, 0))])
                            qev_tok[iq] = tqe
                            qev_last[m] = tqe
                            qdma_tok[qp] = d(SYNC, f"st{2 + qp}",
                                             lambda e, kvh=kvh, g=g, qt=qt, qp=qp:
                                             e.dma_start(
                                out=Q_s[kvh].rearrange(
                                    "p (slot g q) -> p slot g q",
                                    g=G, q=128)[:, qt * 4:(qt + 1) * 4, g, :],
                                in_=s_qev[qp][:].rearrange(
                                    "p (jj q) -> p jj q", jj=4)),
                                waits=[("dve", tqe)])
                    p.flush()
        bar2 = p.barrier_waits()

        p3 = ExitStack()
        with p3:
            s_attn = p3.enter_context(nc.sbuf_tensor("s_attn", [128, HC, OWN], BF16))

            p3a = ExitStack()
            with p3a:
                s_masks = p3a.enter_context(nc.sbuf_tensor(
                    "s_masks", [128, SLOTS * 2 * 128], BF16))
                s_K = [p3a.enter_context(nc.sbuf_tensor(
                    f"s_K{i}", [128, S], BF16)) for i in range(2)]
                s_V = [p3a.enter_context(nc.sbuf_tensor(
                    f"s_V{i}", [128, S], BF16)) for i in range(2)]
                s_Q = [p3a.enter_context(nc.sbuf_tensor(
                    f"s_Q{i}", [128, SLOTS * G * 128 // 2], BF16))
                    for i in range(2)]
                s_P = [p3a.enter_context(nc.sbuf_tensor(
                    f"s_P{i}", [128, 1024], BF16)) for i in range(4)]
                s_l = [p3a.enter_context(nc.sbuf_tensor(
                    f"s_l{i}", [1, 512], F32)) for i in range(3)]
                s_lT = [p3a.enter_context(nc.sbuf_tensor(
                    f"s_lT{i}", [128, 4], F32)) for i in range(3)]
                s_rT = [p3a.enter_context(nc.sbuf_tensor(
                    f"s_rT{i}", [128, 4], F32)) for i in range(3)]

                s_bc = [p3a.enter_context(nc.sbuf_tensor(
                    f"s_bc{i}", [128, 512], F32)) for i in range(3)]
                pp5 = ExitStack()
                p_s = [pp5.enter_context(nc.psum_tensor(
                    f"p_s{i}", [128, 1024], F32)) for i in range(3)]
                p_at = pp5.enter_context(nc.psum_tensor("p_at0", [128, 512], F32))
                p_lb = pp5.enter_context(nc.psum_tensor("p_lb", [128, 512], F32))
                p_l = [p_lb[0:1, :], p_lb[32:33, :], p_lb[64:65, :]]
                with pp5:
                    kv_load = {}
                    qh_tok = {}
                    qk_half_last = {}
                    t_masks = None
                    cg = 0
                    gu = 0
                    exp_tok = {}
                    mask_tok = {}
                    stt_tok = {}
                    bcev_tok = {}
                    lfin_tok = {}
                    evict_tok = {}
                    backlog = []
                    pend_epi = []
                    pend_b = []
                    pend_c = []
                    epi_tr = {}
                    epi_w = {}
                    rcp_tok = {}
                    t3_tok = {}

                    t2_tok = {}
                    t4_tok = {}

                    def emit_epi_a(gu_, kvh_, slot_, up_):
                        lcp = A(lambda e, up=up_, gu=gu_: e.tensor_copy(
                            s_l[up][:], p_l[gu % 3]),
                            waits=[("pe", lfin_tok[gu_]),
                                   ("st0", epi_w.get(gu_ - 3, 0))])
                        epi_tr[gu_] = lcp
                        t1 = d(SYNC, "st0", lambda e, up=up_: e.dma_start(
                            out=bc_dram[up:up + 1, :], in_=s_l[up][:]),
                            waits=[("dve", lcp)])
                        epi_w[gu_] = t1
                        t2_tok[gu_] = d(SYNC, "st1", lambda e, up=up_: e.dma_start(
                            out=s_lT[up][:],
                            in_=bass.AP(bc_dram, up * 512, [[4, 128], [1, 4]])),
                            waits=[("st0", t1),
                                   ("dve", rcp_tok.get(gu_ - 3, 0))])

                    def emit_epi_b(gu_, kvh_, slot_, up_):
                        r = A(lambda e, up=up_: e.reciprocal(
                            s_rT[up][:], s_lT[up][:]),
                            waits=[("st1", t2_tok[gu_]),
                                   ("st0", t3_tok.get(gu_ - 3, 0))])
                        rcp_tok[gu_] = r
                        t3 = d(SYNC, "st0", lambda e, up=up_: e.dma_start(
                            out=bass.AP(bc_dram2, up * 512, [[4, 128], [1, 4]]),
                            in_=s_rT[up][:]),
                            waits=[("dve", r)])
                        t3_tok[gu_] = t3
                        t4_tok[gu_] = d(SYNC, "st1", lambda e, up=up_: e.dma_start(
                            out=s_bc[up][:],
                            in_=bass.AP(bc_dram2, up * 512, [[0, 128], [1, 512]])),
                            waits=[("st0", t3),
                                   ("dve", stt_tok.get(gu_ - 3, 0))])

                    def emit_epi_c(gu_, kvh_, slot_, up_):
                        stt_tok[gu_] = A(lambda e, up=up_, kvh=kvh_, slot=slot_:
                                         e.scalar_tensor_tensor(
                            s_attn[:, kvh * G:(kvh + 1) * G,
                                   slot * 128:(slot + 1) * 128],
                            s_attn[:, kvh * G:(kvh + 1) * G,
                                   slot * 128:(slot + 1) * 128],
                            1.0,
                            s_bc[up][:].rearrange("p (i j) -> p i j", i=G),
                            ALU.bypass, ALU.mult),
                            waits=[("st1", t4_tok[gu_]),
                                   ("dve", evict_tok[gu_])])

                    def emit_rest(fp_, kvh_, kp_, slot_, pj_, C_, gu_, up_,
                                  ms_, msp_, tqk_):
                        ps = fp_ % 3
                        sp = fp_ % 4
                        cc0 = 2 * pj_
                        te = p.add(ACT, lambda e, ps=ps, sp=sp: e.activation(
                            s_P[sp][:], p_s[ps][:], AF.Exp, scale=ISCALE),
                            waits=[("pe", tqk_)])
                        exp_tok[fp_] = te
                        dep = ("act", te)
                        if cc0 == C_ - 2:
                            tm = None
                            for gg_ in range(G):
                                tm = A(lambda e, sp=sp, gg_=gg_, slot=slot_:
                                       e.tensor_tensor(
                                    s_P[sp][:].rearrange(
                                        "p (c gq) -> p c gq", c=2)
                                    [:, :, gg_ * 128:(gg_ + 1) * 128],
                                    s_P[sp][:].rearrange(
                                        "p (c gq) -> p c gq", c=2)
                                    [:, :, gg_ * 128:(gg_ + 1) * 128],
                                    s_masks[:, slot * 256:(slot + 1) * 256]
                                    .rearrange("p (c q) -> p c q", c=2),
                                    ALU.mult),
                                    waits=[("act", te), ("qld", t_masks)]
                                    if gg_ == 0 else (),
                                    inc="auto" if gg_ == G - 1 else None)
                            mask_tok[fp_] = tm
                            dep = ("dve", tm)
                        for h_ in range(2):
                            cc = cc0 + h_
                            aw = [dep] if h_ == 0 else []
                            if cc == 0:
                                aw.append(("dve", evict_tok.get(gu_ - 1, 0)))
                            p.add(PE, lambda e, cc=cc, sp=sp, kp=kp_,
                                  h_=h_: e.matmul(
                                p_at[:], s_V[kp][:, cc * 128:(cc + 1) * 128],
                                s_P[sp][:, h_ * 512:(h_ + 1) * 512],
                                start=(cc == 0), stop=(cc == C_ - 1),
                                skip_group_check=True),
                                waits=aw)
                        for h_ in range(2):
                            cc = cc0 + h_
                            ll = p.add(PE, lambda e, cc=cc, sp=sp, gu=gu_,
                                       h_=h_: e.matmul(
                                p_l[gu % 3], s_onesb[:],
                                s_P[sp][:, h_ * 512:(h_ + 1) * 512],
                                start=(cc == 0), stop=(cc == C_ - 1),
                                skip_group_check=True),
                                waits=[("dve", epi_tr.get(gu_ - 3, 0))]
                                if cc == 0 else (),
                                inc=("pe", 1) if cc == C_ - 1 else None)
                        if cc0 == C_ - 2:
                            lfin_tok[gu_] = ll
                            evict_tok[gu_] = A(
                                lambda e, kvh=kvh_, slot=slot_:
                                e.tensor_copy(
                                    s_attn[:, kvh * G:(kvh + 1) * G,
                                           slot * 128:(slot + 1) * 128],
                                    p_at[:].rearrange(
                                        "p (i j) -> p i j", i=G)),
                                waits=[("pe", ll)])
                            pend_epi.append((gu_, kvh_, slot_, up_, fp_))

                    fp = 0
                    for kvh in range(KVH):
                        kp = kvh % 2
                        wts = list(bar2) if kvh == 0 else \
                            [("pe", lfin_tok.get((kvh - 1) * SLOTS - 1, 0))]
                        d(POOL, f"qkv{kp}", lambda e, kvh=kvh, kp=kp: e.dma_start(
                            out=s_K[kp][:, 0:S // 2],
                            in_=kv_full[0, kvh, :, 0:S // 2]),
                            waits=wts + [("cc", 1)])
                        d(POOL, f"qkv{kp}", lambda e, kvh=kvh, kp=kp: e.dma_start(
                            out=s_K[kp][:, S // 2:S],
                            in_=kv_full[1, kvh, :, 0:S // 2]))
                        d(POOL, f"qkv{kp}", lambda e, kvh=kvh, kp=kp: e.dma_start(
                            out=s_V[kp][:, 0:S // 2],
                            in_=kv_full[0, kvh, :, S // 2:S]))
                        d(POOL, f"qkv{kp}", lambda e, kvh=kvh, kp=kp: e.dma_start(
                            out=s_V[kp][:, S // 2:S],
                            in_=kv_full[1, kvh, :, S // 2:S]))
                        QH = SLOTS * G * 128 // 2
                        for h2 in (1, 0):
                            qh_tok[(kvh, h2)] = d(
                                POOL, f"qkv{kp}",
                                lambda e, kvh=kvh, h2=h2, QH=QH: e.dma_start(
                                    out=s_Q[h2][:],
                                    in_=Q_s[kvh][:, h2 * QH:(h2 + 1) * QH]),
                                waits=[("pe", qk_half_last.get(
                                    (kvh - 1, h2), 0))])
                        if t_masks is None:
                            t_masks = d(SYNC, "qld", lambda e: e.dma_start(
                                out=s_masks[:], in_=msel_d[:]), waits=bar2)
                        slot_order = []
                        for half in (1, 0):
                            lo = half * (SLOTS // 2)
                            hi = lo + SLOTS // 2 - 1
                            while lo <= hi:
                                slot_order.append(hi)
                                if lo < hi:
                                    slot_order.append(lo)
                                lo, hi = lo + 1, hi - 1
                        for slot in slot_order:
                            C = 2 * (slot + 1)
                            ms = kvh * SLOTS + slot
                            for pj in range(C // 2):
                                ps = fp % 3
                                up = gu % 3
                                while pend_epi and fp - pend_epi[0][4] >= 1:
                                    e_ = pend_epi.pop(0)
                                    emit_epi_a(*e_[:4])
                                    pend_b.append(e_)
                                while pend_b and fp - pend_b[0][4] >= 4:
                                    e_ = pend_b.pop(0)
                                    emit_epi_b(*e_[:4])
                                    pend_c.append(e_)
                                while pend_c and fp - pend_c[0][4] >= 8:
                                    emit_epi_c(*pend_c.pop(0)[:4])
                                if len(backlog) >= 3:
                                    emit_rest(*backlog.pop(0))
                                wts = [("act", exp_tok.get(fp - 3, 0)),
                                       (f"qkv{kp}",
                                        qh_tok[(kvh, slot // (SLOTS // 2))])]
                                if (fp - 4) in mask_tok:
                                    wts.append(("dve", mask_tok[fp - 4]))
                                tqk = None
                                for h_ in range(2):
                                    cc = 2 * pj + h_
                                    tqk = p.add(PE, lambda e, cc=cc, slot=slot,
                                                ps=ps, kp=kp, h_=h_: e.matmul(
                                        p_s[ps][:, h_ * 512:(h_ + 1) * 512],
                                        s_K[kp][:, cc * 128:(cc + 1) * 128],
                                        s_Q[slot // (SLOTS // 2)]
                                        [:, (slot % (SLOTS // 2)) * 512:
                                         (slot % (SLOTS // 2) + 1) * 512],
                                        start=True, stop=True,
                                        skip_group_check=True),
                                        waits=wts if h_ == 0 else (),
                                        inc=("pe", 1) if h_ == 1 else None)
                                qk_half_last[(kvh, slot // (SLOTS // 2))] = tqk
                                backlog.append((fp, kvh, kp, slot, pj, C, gu,
                                                up, ms, 0, tqk))
                                fp += 1
                            gu += 1
                    while backlog:
                        emit_rest(*backlog.pop(0))
                    while pend_epi:
                        e_ = pend_epi.pop(0)
                        emit_epi_a(*e_[:4])
                        pend_b.append(e_)
                    while pend_b:
                        e_ = pend_b.pop(0)
                        emit_epi_b(*e_[:4])
                        pend_c.append(e_)
                    while pend_c:
                        emit_epi_c(*pend_c.pop(0)[:4])
                    attn_done = p.n["dve"]
                    if DEBUG_SCRATCH:
                        d(SYNC, "qld", lambda e: e.dma_start(
                            out=dbg_attn[:],
                            in_=s_attn[:].rearrange("p a b -> p (a b)")),
                            waits=[("dve", attn_done)])
                    p.flush()
            bar3 = p.barrier_waits()

            p4 = ExitStack()
            with p4:
                s_wo = [p4.enter_context(nc.sbuf_tensor(
                    f"s_wo{i}", [128, HC * 512], BF16)) for i in range(2)]
                s_g4 = [p4.enter_context(nc.sbuf_tensor(
                    f"s_g4_{i}", [128, 512], F32)) for i in range(2)]
                s_x4 = [p4.enter_context(nc.sbuf_tensor(
                    f"s_x4_{i}", [128, 512], F32)) for i in range(2)]
                s_o4 = [p4.enter_context(nc.sbuf_tensor(
                    f"s_o4_{i}", [128, 512], F32)) for i in range(2)]
                pp6 = ExitStack()
                p_y4 = [pp6.enter_context(nc.psum_tensor(
                    f"p_y4{i}", [128, 512], F32)) for i in range(2)]
                with pp6:
                    wo_tok = {}
                    pe_ot_done = {}
                    sg_tok = {}
                    out_tok = {}
                    add_tok = {}
                    add_last = {}
                    for ot in range(NOT):
                        op2 = ot % 2
                        wo_tok[ot] = d(POOL, f"qw{op2}", lambda e, ot=ot, op2=op2:
                                       e.dma_start(
                            out=s_wo[op2][:],
                            in_=wo_t[ot][:].rearrange("p a b -> p (a b)")),
                            waits=(list(bar3) if ot == 0 else []) +
                                  [("dve", add_last.get(ot - 2, 0))])
                        for t in range(TCH):
                            tp = t % 2
                            kk = ot * TCH + t
                            tg = d(SYNC, f"st{tp}", lambda e, t=t, ot=ot, tp=tp:
                                   e.dma_start(
                                out=s_g4[tp][:],
                                in_=g_s[t * 128:(t + 1) * 128,
                                        ot * 512:(ot + 1) * 512]),
                                waits=(list(bar3) if kk == 0 else []) +
                                      [("dve", sg_tok.get(kk - 2, 0))])
                            txl = d(SYNC, f"st{tp}", lambda e, t=t, ot=ot, tp=tp:
                                    e.dma_start(
                                out=s_x4[tp][:],
                                in_=x_own[t * 128:(t + 1) * 128,
                                          ot * 512:(ot + 1) * 512]),
                                waits=[("dve", add_tok.get(kk - 2, 0))])
                            wts = [(f"qw{op2}", wo_tok[ot]), ("dve", attn_done),
                                   ("dve", add_tok.get(kk - 2, 0))]
                            if kk == 0:
                                wts = wts + list(bar3)
                            lastmm = None
                            for fc in range(HC):
                                lastmm = p.add(
                                    PE, lambda e, t=t, fc=fc, op2=op2, tp=tp:
                                    e.matmul(
                                        p_y4[tp][:],
                                        s_attn[:, fc, t * 128:(t + 1) * 128],
                                        s_wo[op2][:, fc * 512:(fc + 1) * 512],
                                        start=(fc == 0), stop=(fc == HC - 1)),
                                    waits=wts if fc == 0 else (),
                                    inc=("pe", 1) if fc == HC - 1 else None)
                            pe_ot_done[ot] = lastmm
                            tsg = A(lambda e, tp=tp: e.tensor_tensor(
                                p_y4[tp][:], p_y4[tp][:], s_g4[tp][:], ALU.mult),
                                waits=[("pe", lastmm), (f"st{tp}", txl)])
                            sg_tok[kk] = tsg
                            tadd = A(lambda e, tp=tp: e.tensor_tensor(
                                s_o4[tp][:], p_y4[tp][:], s_x4[tp][:], ALU.add),
                                waits=[(f"st{tp}", txl),
                                       (f"st{2 + tp}", out_tok.get(kk - 2, 0))],
                                drain=True)
                            add_tok[kk] = tadd
                            add_last[ot] = tadd
                            out_tok[kk] = d(ACT, f"st{2 + tp}",
                                            lambda e, t=t, ot=ot, tp=tp: e.dma_start(
                                out=out_rows[t * 128:(t + 1) * 128,
                                             ot * 512:(ot + 1) * 512],
                                in_=s_o4[tp][:]), waits=[("dve", tadd)])
                    p.add(ACT, lambda e: e.nop(),
                          waits=[("st2", p.n["st2"]), ("st3", p.n["st3"])],
                          inc=None)
                    p.flush()
    return nc


def _stage_weights(w_q, w_k, w_v, w_o, W, bias, A, B, cfg):
    c = cfg_derived(cfg)
    D, FC, NOT, HC, DKV = c["D"], c["FC"], c["NOT"], c["HC"], c["DKV"]
    f32 = np.float32
    wk_t = np.ascontiguousarray(w_k.reshape(FC, 128, DKV).transpose(1, 0, 2))
    wv_t = np.ascontiguousarray(w_v.reshape(FC, 128, DKV).transpose(1, 0, 2))
    wq_t = np.ascontiguousarray(
        w_q.reshape(FC, 128, HC, 128).transpose(2, 1, 0, 3))
    wf_t = np.ascontiguousarray(
        W.reshape(NOT, 512, FC, 128).transpose(0, 3, 2, 1))
    wo_t = np.ascontiguousarray(
        w_o.reshape(HC, 128, NOT, 512).transpose(2, 1, 0, 3))
    bias_bc = np.ascontiguousarray(np.broadcast_to(bias.astype(f32), (128, D)))
    a_st = np.ascontiguousarray(
        A.reshape(3, 16, FC, 128).transpose(3, 0, 2, 1).reshape(128, 3, FC * 16))
    Do8 = D // 8
    p_idx = np.arange(128)
    r_idx = p_idx // 8
    ob_idx = p_idx % 8
    b_st = np.stack([B[n][(ob_idx[:, None] * Do8 + np.arange(Do8)[None, :]),
                          r_idx[:, None]] for n in range(3)], axis=1)
    bf16 = ml_dtypes.bfloat16
    return dict(wk_t=wk_t.astype(bf16),
                wv_t=wv_t.astype(bf16),
                wq_t=wq_t.astype(bf16),
                wf_t=wf_t.astype(bf16),
                wo_t=wo_t.astype(bf16),
                bias_bc=bias_bc, a_st=a_st.astype(bf16),
                b_st=np.ascontiguousarray(b_st).astype(bf16))


def _stage_masks(parity, cfg):
    c = cfg_derived(cfg)
    SLOTS, NBLK, G = c["SLOTS"], c["NBLK"], c["G"]
    blocks = own_blocks(parity, NBLK)
    tri = (np.arange(128)[:, None] <= np.arange(128)[None, :]).astype(np.float32)
    ones = np.ones((128, 128), np.float32)
    zeros = np.zeros((128, 128), np.float32)
    m = np.empty((SLOTS, 2, 128, 128), np.float32)
    for s in range(SLOTS):
        n = blocks[s] + 1
        C = 2 * (s + 1)
        if n == C:
            m[s, 0] = ones
            m[s, 1] = tri
        else:
            assert n == C - 1
            m[s, 0] = tri
            m[s, 1] = zeros
    m = m.transpose(2, 0, 1, 3).reshape(128, SLOTS * 2 * 128)
    return np.ascontiguousarray(m).astype(ml_dtypes.bfloat16)


def kernel(L0_attn_hidden_states, L0_attn_w_q, L0_attn_w_k, L0_attn_w_v,
           L0_attn_w_o, L0_ffn_weight, L0_ffn_bias, L0_ffn_lora_A,
           L0_ffn_lora_B, L0_ffn_lora_start_indices, _cfg=None):
    cfg = _cfg or REAL_CFG
    c = cfg_derived(cfg)
    B, S, D, NBLK = c["B"], c["S"], c["D"], c["NBLK"]
    x = np.asarray(L0_attn_hidden_states, dtype=np.float32)
    shared = _stage_weights(np.asarray(L0_attn_w_q, np.float32),
                            np.asarray(L0_attn_w_k, np.float32),
                            np.asarray(L0_attn_w_v, np.float32),
                            np.asarray(L0_attn_w_o, np.float32),
                            np.asarray(L0_ffn_weight, np.float32),
                            np.asarray(L0_ffn_bias, np.float32),
                            np.asarray(L0_ffn_lora_A, np.float32),
                            np.asarray(L0_ffn_lora_B, np.float32), cfg)
    lsi = np.asarray(L0_ffn_lora_start_indices, np.int32).reshape(1, 4)
    ident = np.eye(128, dtype=ml_dtypes.bfloat16)
    onesb = np.ones((128, 1), dtype=ml_dtypes.bfloat16)
    onesrow = np.ones((1, 128), dtype=np.float32)
    onesrowb = np.ones((1, 128), dtype=ml_dtypes.bfloat16)
    consts = np.zeros((128, 8), np.float32)
    consts[:, 0] = c["EPS"]
    masks = {par: _stage_masks(par, cfg) for par in (0, 1)}

    n_cores = 2 * B
    in_maps = []
    for core in range(n_cores):
        b, par = core // 2, core % 2
        blocks = own_blocks(par, NBLK)
        perm = np.concatenate([np.arange(j * 128, (j + 1) * 128) for j in blocks])
        m = dict(shared)
        m["x_half"] = np.ascontiguousarray(
            x[b][par * (S // 2):(par + 1) * (S // 2)])
        m["x_own"] = np.ascontiguousarray(x[b][perm])
        m["lsi"] = lsi
        m["bidx"] = np.array([[b]], dtype=np.int32)
        m["ident"] = ident
        m["onesb"] = onesb
        m["onesrow"] = onesrow
        m["onesrowb"] = onesrowb
        m["constpool"] = consts
        m["msel"] = masks[par]
        in_maps.append(m)

    nc = build_nc(cfg)
    res = run_bass_kernel_spmd(nc, in_maps, core_ids=list(range(n_cores)))
    out = np.empty((B, S, D), dtype=np.float32)
    for core in range(n_cores):
        b, par = core // 2, core % 2
        rows = np.asarray(res.results[core]["out_rows"])
        blocks = own_blocks(par, NBLK)
        for s_i, j in enumerate(blocks):
            out[b, j * 128:(j + 1) * 128] = rows[s_i * 128:(s_i + 1) * 128]
    return out
